# version 01
# baseline (speedup 1.0000x reference)
import math
from contextlib import ExitStack

import numpy as np
import ml_dtypes

import concourse.bass as bass
import concourse.mybir as mybir
from concourse.bass_utils import run_bass_kernel_spmd

bf16 = ml_dtypes.bfloat16
F32 = mybir.dt.float32
BF16 = mybir.dt.bfloat16
I32 = mybir.dt.int32
U8 = mybir.dt.uint8
AF = mybir.ActivationFunctionType
ALU = mybir.AluOpType

D = 4096
I = 11008
S = 8192
NCORES = 8
HALO = 256
CEN = 1024
MLOC = CEN + 2 * HALO
H = 32
DH = 128
KT = D // 128
NT = I // 128
G = 128
NG_GU = D // G
NG_DN = I // G
SCALE = 1.0 / math.sqrt(DH)
EPS = 1e-5
MASKNEG = -30.0

_INV = np.array([0, 2, 4, 6, 1, 3, 5, 7])


def _sigma(n):
    p = np.arange(128)
    h = p // 64
    c = (p % 64) // 4
    b = p % 4
    base = 8 * c + _INV[2 * b + h]
    out = np.empty(n, dtype=np.int64)
    for t in range(n // 128):
        out[t * 128:(t + 1) * 128] = 128 * t + base
    return out


SIGMA_D = _sigma(D)
SIGMA_I = _sigma(I)


def _stage_wbytes(qweight, sigma_k):
    q = np.ascontiguousarray(np.asarray(qweight)[sigma_k, :])
    b = q.view(np.uint8).reshape(q.shape[0], q.shape[1], 4)
    return np.ascontiguousarray(b.transpose(1, 2, 0))


def _stage_zbytes(qzeros):
    z = np.ascontiguousarray(np.asarray(qzeros).T)
    b = z.view(np.uint8).reshape(z.shape[0], z.shape[1], 4)
    return np.ascontiguousarray(b.transpose(0, 2, 1))


class Sem:
    def __init__(self, ctx, nc, name):
        self.h = ctx.enter_context(nc.semaphore(name))
        self.n = 0

    def inc(self, instr, k=1):
        instr.then_inc(self.h, k)
        self.n += k
        return self.n

    def dma(self, instr):
        return self.inc(instr, 16)


class Ring:

    def __init__(self, sem, name, n):
        self.s = [sem(f"{name}{j}") for j in range(n)]
        self.n = n

    def dma(self, slot, instr):
        return self.s[slot % self.n].dma(instr)

    def h(self, slot):
        return self.s[slot % self.n].h

    def fin(self, eng):
        for s in self.s:
            eng.wait_ge(s.h, s.n)


class Arena:
    def __init__(self, t, cols):
        self.t = t
        self.cols = cols

    def bf(self, off, n):
        assert off + n <= self.cols, (off, n)
        return self.t[:, off:off + n]

    def f32(self, off, n):
        assert off % 2 == 0 and off + 2 * n <= self.cols, (off, n)
        return self.t.bitcast(F32)[:, off // 2:off // 2 + n]

    def i32(self, off, n):
        assert off % 2 == 0 and off + 2 * n <= self.cols, (off, n)
        return self.t.bitcast(I32)[:, off // 2:off // 2 + n]


ARENA_COLS = 99328
P0_CVT = [0, 8192]
P0_WB = [16384, 16384 + 22016]
P0_WO = [16384 + 44032, 16384 + 44032 + 11008]
_z0 = 16384 + 44032 + 22016
P0_ZB = [_z0, _z0 + 172]
P0_ZF = [_z0 + 344, _z0 + 344 + 172]
P0_SP = [_z0 + 688, _z0 + 688 + 172]
P0_BI = [_z0 + 1032, _z0 + 1032 + 172]
P0_CVD = [_z0 + 1376, _z0 + 1376 + 4096]
F_XLN = 0
F_XB = 49152
F_SQ = 65536
F_RB = 66560
F_TMP = 67584
F_WGW = 68608
F_WUW = 85000
F_WUW = 68608 + 16384
F_SG = 84992 + 16384
F_WGW = 49152 + 16384
F_WGW = F_XB
F_WUW = F_WGW + 16384
F_SG = F_WUW + 16384
F_TT = F_SG + 1024
F_HB = F_TT + 1024
F_SQ = 86016
F_RB = 87040
F_TMP = 88064
F_HBUF = 0
F_WDS = 44032
F_XT = 88576
F_X1 = F_XT + 4096
F_OROW = 0
F_OROW = 66048
F_OROW = None

A_XLN2 = 0
A_QRT = 16384
A_ATT = 49152
A_KRT = 81920
A_VSB = 83456
A_COS = 84992
A_SIN = 86528
A_KS = 88064
A_PRB = 89088
A_WQS = 91136
A_FB = 99328 - 512
A_WOS = A_WQS
A_QX = A_XLN2


def build_program(stages=("p0", "ffn1", "attn", "ffn2"), dbg=None, dbg_weights=False, p0_only=None, attn_lvl=4):
    nc = bass.Bass("TRN2", target_bir_lowering=False)
    dt = nc.dram_tensor

    tin = {}
    tin["xt"] = dt("xt", [D, MLOC], F32, kind="ExternalInput").ap()
    tin["qgb"] = dt("qgb", [I // 8, 4, D], U8, kind="ExternalInput").ap()
    tin["qub"] = dt("qub", [I // 8, 4, D], U8, kind="ExternalInput").ap()
    tin["qdb"] = dt("qdb", [D // 8, 4, I], U8, kind="ExternalInput").ap()
    tin["zgb"] = dt("zgb", [I // 8, 4, NG_GU], U8, kind="ExternalInput").ap()
    tin["zub"] = dt("zub", [I // 8, 4, NG_GU], U8, kind="ExternalInput").ap()
    tin["zdb"] = dt("zdb", [D // 8, 4, NG_DN], U8, kind="ExternalInput").ap()
    tin["sg"] = dt("sg", [I, NG_GU], F32, kind="ExternalInput").ap()
    tin["su"] = dt("su", [I, NG_GU], F32, kind="ExternalInput").ap()
    tin["sd"] = dt("sd", [D, NG_DN], F32, kind="ExternalInput").ap()
    tin["wq"] = dt("wq", [D, D], F32, kind="ExternalInput").ap()
    tin["wk"] = dt("wk", [D, DH], F32, kind="ExternalInput").ap()
    tin["wv"] = dt("wv", [D, DH], F32, kind="ExternalInput").ap()
    tin["wo"] = dt("wo", [D, D], F32, kind="ExternalInput").ap()
    tin["cosd"] = dt("cosd", [128, MLOC], F32, kind="ExternalInput").ap()
    tin["sind"] = dt("sind", [128, MLOC], F32, kind="ExternalInput").ap()
    tin["rmat"] = dt("rmat", [128, 128], BF16, kind="ExternalInput").ap()
    tin["idm"] = dt("idm", [128, 128], F32, kind="ExternalInput").ap()
    tin["alphav"] = dt("alphav", [1, 4 * 256], F32, kind="ExternalInput").ap()
    tin["almv"] = dt("almv", [1, 4 * 256], F32, kind="ExternalInput").ap()
    tin["smaskv"] = dt("smaskv", [20, 128], F32, kind="ExternalInput").ap()

    out = dt("outq", [CEN, D], F32, kind="ExternalOutput").ap()

    wkind = "ExternalOutput" if dbg_weights else "Internal"
    scr = {}
    scr["wgT"] = dt("wgT", [I, D], BF16, kind=wkind).ap()
    scr["wuT"] = dt("wuT", [I, D], BF16, kind=wkind).ap()
    scr["wdT"] = dt("wdT", [D, I], BF16, kind=wkind).ap()
    scr["wdN"] = dt("wdN", [16, 128, NT * 256], BF16, kind=wkind).ap()
    scr["wqb"] = dt("wqb", [KT, D, 128], BF16, kind="Internal").ap()
    scr["wob"] = dt("wob", [KT, D, 128], BF16, kind="Internal").ap()
    scr["wkb"] = dt("wkb", [D, DH], BF16, kind="Internal").ap()
    scr["wvb"] = dt("wvb", [D, DH], BF16, kind="Internal").ap()
    scr["x1h"] = dt("x1h", [D, MLOC], F32, kind="Internal").ap()
    scr["x2h"] = dt("x2h", [D, CEN], F32, kind="Internal").ap()
    scr["xlnh"] = dt("xlnh", [D, MLOC], BF16, kind="Internal").ap()
    scr["hh"] = dt("hh", [3, 128, NT * 512], BF16, kind=wkind).ap()

    dbg_map = {}
    if dbg:
        for name, shp, dtp in dbg:
            dbg_map[name] = dt(name, shp, dtp, kind="ExternalOutput").ap()

    ctx = ExitStack()
    with ctx:
        A = Arena(ctx.enter_context(nc.sbuf_tensor("arena", [128, ARENA_COLS], BF16)),
                  ARENA_COLS)
        sb = {}
        sb["maskc"] = ctx.enter_context(nc.sbuf_tensor("maskc", [128, 1], I32))
        sb["hisc"] = ctx.enter_context(nc.sbuf_tensor("hisc", [128, 1], F32))
        sb["epsc"] = ctx.enter_context(nc.sbuf_tensor("epsc", [128, 1], F32))
        sb["ones_b"] = ctx.enter_context(nc.sbuf_tensor("ones_b", [128, 1], BF16))
        sb["ones_r"] = ctx.enter_context(nc.sbuf_tensor("ones_r", [1, 128], F32))
        sb["ones_rb"] = ctx.enter_context(nc.sbuf_tensor("ones_rb", [1, 128], BF16))
        sb["rmat_s"] = ctx.enter_context(nc.sbuf_tensor("rmat_s", [128, 128], BF16))
        sb["idm_s"] = ctx.enter_context(nc.sbuf_tensor("idm_s", [128, 128], F32))
        sb["smask_s"] = ctx.enter_context(nc.sbuf_tensor("smask_s", [128, 20], F32))
        sb["alpha_s"] = A.f32(94720, 1024)[0:1, :]
        sb["alm_s"] = A.f32(96768, 1024)[0:1, :]
        sb["frow"] = A.bf(98816, 512)[0:1, :]
        ps = [ctx.enter_context(nc.psum_tensor(f"pp{i}", [128, 512], F32)) for i in range(8)]

        sems = {}

        def sem(name):
            if name not in sems:
                sems[name] = Sem(ctx, nc, name)
            return sems[name]

        block = ctx.enter_context(nc.Block())
        env = dict(nc=nc, block=block, sem=sem, A=A, sb=sb, ps=ps, tin=tin,
                   scr=scr, out=out, dbg=dbg_map, p0_only=p0_only)

        PH = sem("PH")
        phase_no = [0]

        def barrier():
            phase_no[0] += 1
            target = 5 * phase_no[0]
            for e in ["tensor", "vector", "scalar", "sync", "gpsimd"]:
                def _f(eng):
                    if e in ("vector", "scalar", "gpsimd"):
                        eng.drain()
                    eng.sem_inc(PH.h, 1)
                    eng.wait_ge(PH.h, target)
                getattr(block, e)(_f)
            PH.n += 5

        _emit_consts(env)
        barrier()
        if "p0" in stages:
            _emit_p0(env)
            barrier()
        if "ffn1" in stages:
            dst = dbg_map.get("dx1", scr["x1h"])
            _emit_ffn(env, src_x=tin["xt"], dst_x=dst, m0=0, m1=MLOC, final=False)
            barrier()
        if "attn" in stages:
            dst2 = dbg_map.get("dx2", scr["x2h"])
            _emit_attn(env, dst_x2=dst2, lvl=attn_lvl)
            barrier()
        if "ffn2" in stages:
            _emit_ffn(env, src_x=scr["x2h"], dst_x=None, m0=HALO, m1=HALO + CEN,
                      final=True)
            barrier()

    return nc



def _emit_consts(env):
    block, sem, sb, tin = env["block"], env["sem"], env["sb"], env["tin"]
    sMisc = sem("sMisc")

    def _v(v):
        v.memset(sb["maskc"][0:64, :], 15)
        v.memset(sb["maskc"][64:128, :], 240)
        v.memset(sb["hisc"][0:64, :], 1.0)
        v.memset(sb["hisc"][64:128, :], 1.0 / 16.0)
        v.memset(sb["epsc"][:, :], EPS)
        v.memset(sb["ones_b"][:, :], 1.0)
        v.memset(sb["ones_r"][:, :], 1.0)
        v.memset(sb["ones_rb"][:, :], 1.0)
    block.vector(_v)

    def _s(sy):
        sMisc.dma(sy.dma_start(out=sb["rmat_s"][:, :], in_=tin["rmat"][:, :]))
        sMisc.dma(sy.dma_start(out=sb["idm_s"][:, :], in_=tin["idm"][:, :]))
        sMisc.dma(sy.dma_start(out=sb["alpha_s"][:, :], in_=tin["alphav"][:, :]))
        sMisc.dma(sy.dma_start(out=sb["alm_s"][:, :], in_=tin["almv"][:, :]))
        sMisc.dma(sy.dma_start(out=sb["smask_s"][:, :],
                               in_=tin["smaskv"].rearrange("r p -> p r"),
                               allow_slow_non_contiguous=True))
        sy.wait_ge(sMisc.h, sMisc.n)
    block.sync(_s)



def _emit_p0(env):
    nc, block, sem, A, sb, tin, scr = (env["nc"], env["block"], env["sem"],
                                       env["A"], env["sb"], env["tin"], env["scr"])
    rCv = Ring(sem, "rCv", 2)
    rCw = Ring(sem, "rCw", 2)
    rGd = Ring(sem, "rGd", 2)
    rSt = Ring(sem, "rSt", 2)
    vPrep = sem("vPrep")
    vAff = sem("vAff")
    aAff = sem("aAff")
    rWr = Ring(sem, "rWr", 2)
    dnW = sem("dnW")
    rCd = Ring(sem, "rCd", 2)
    rCd2 = Ring(sem, "rCd2", 2)

    mats = [
        ("dn", tin["qdb"], tin["zdb"], tin["sd"], scr["wdT"], D // 128, I, NG_DN),
        ("gt", tin["qgb"], tin["zgb"], tin["sg"], scr["wgT"], NT, D, NG_GU),
        ("up", tin["qub"], tin["zub"], tin["su"], scr["wuT"], NT, D, NG_GU),
    ]
    only = env.get("p0_only")
    if only:
        mats = [m for m in mats if m[0] == only[0]]
        mats = [(m[0], m[1], m[2], m[3], m[4], min(m[5], only[1]), m[6], m[7]) for m in mats]
    tiles = []
    for mi, m in enumerate(mats):
        for ti in range(m[5]):
            tiles.append((mi, ti, m[6], m[7]))
    ntot = len(tiles)
    DNT = D // 128

    jobs = []
    for at in range(KT):
        jobs.append((tin["wq"][:, at * 128:(at + 1) * 128], scr["wqb"][at]))
    for at in range(KT):
        jobs.append((tin["wo"][:, at * 128:(at + 1) * 128], scr["wob"][at]))
    jobs.append((tin["wk"][:, :], scr["wkb"]))
    jobs.append((tin["wv"][:, :], scr["wvb"]))
    if env.get("p0_only"):
        jobs = []

    def _g(g):
        for i, (mi, ti, K, NGk) in enumerate(tiles):
            qb, zb = mats[mi][1], mats[mi][2]
            if i >= 2:
                g.wait_ge(vAff.h, i - 1)
                g.wait_ge(aAff.h, i - 1)
            wb = A.i32(P0_WB[i % 2], K)
            zbuf = A.i32(P0_ZB[i % 2], NGk)
            qsr = qb[16 * ti:16 * ti + 16].rearrange("c b k -> (c b) k")
            rGd.dma(i, g.dma_start(out=wb[0:64, :], in_=qsr))
            rGd.dma(i, g.dma_start(out=wb[64:128, :], in_=qsr))
            zsr = zb[16 * ti:16 * ti + 16].rearrange("c b g -> (c b) g")
            rGd.dma(i, g.dma_start(out=zbuf[0:64, :], in_=zsr))
            rGd.dma(i, g.dma_start(out=zbuf[64:128, :], in_=zsr))
        for j, (src, dst) in enumerate(jobs):
            if j >= 2:
                g.wait_ge(rCw.h(j), 16 * (j // 2))
            cvt = A.bf(P0_CVT[j % 2], 4096)
            src_r = src.rearrange("(kt p) a -> p kt a", p=128)
            rCv.dma(j, g.dma_start(out=cvt[:, 0:src_r.shape[1] * src_r.shape[2]], in_=src_r))
        rGd.fin(g)
        rCv.fin(g)
    block.gpsimd(_g)

    def _v(v):
        for i, (mi, ti, K, NGk) in enumerate(tiles):
            wb = A.i32(P0_WB[i % 2], K)
            zbuf = A.i32(P0_ZB[i % 2], NGk)
            zf = A.f32(P0_ZF[i % 2], NGk)
            sp = A.f32(P0_SP[i % 2], NGk)
            bi = A.f32(P0_BI[i % 2], NGk)
            wout = A.bf(P0_WO[i % 2], K)
            v.wait_ge(rGd.h(i), 64 * (i // 2 + 1))
            v.wait_ge(rSt.h(i), 16 * (i // 2 + 1))
            if i >= 2:
                v.wait_ge(rWr.h(i), 16 * (i // 2))
                v.wait_ge(aAff.h, i - 1)
            v.tensor_single_scalar(wb[:, :], wb[:, :], sb["maskc"][:, 0:1], ALU.bitwise_and)
            v.tensor_single_scalar(zbuf[:, :], zbuf[:, :], sb["maskc"][:, 0:1], ALU.bitwise_and)
            v.tensor_single_scalar(sp[:, :], sp[:, :], sb["hisc"][:, 0:1], ALU.mult)
            v.drain()
            v.tensor_copy(zf[:, :], zbuf[:, :])
            v.drain()
            v.tensor_mul(bi[:, :], zf[:, :], sp[:, :])
            v.drain()
            v.tensor_single_scalar(bi[:, :], bi[:, :], -1.0, ALU.mult)
            vPrep.inc(v.drain())
            half = NGk // 2
            for g_ in range(half, NGk):
                v.tensor_scalar(wout[:, g_ * G:(g_ + 1) * G], wb[:, g_ * G:(g_ + 1) * G],
                                zf[:, g_:g_ + 1], sp[:, g_:g_ + 1], ALU.subtract, ALU.mult)
            vAff.inc(v.drain())
    block.vector(_v)

    def _a(a):
        for i, (mi, ti, K, NGk) in enumerate(tiles):
            wb = A.i32(P0_WB[i % 2], K)
            sp = A.f32(P0_SP[i % 2], NGk)
            bi = A.f32(P0_BI[i % 2], NGk)
            wout = A.bf(P0_WO[i % 2], K)
            a.wait_ge(vPrep.h, i + 1)
            if i >= 2:
                a.wait_ge(rWr.h(i), 16 * (i // 2))
            half = NGk // 2
            for g_ in range(half):
                a.activation(out=wout[:, g_ * G:(g_ + 1) * G], in_=wb[:, g_ * G:(g_ + 1) * G],
                             func=AF.Identity, bias=bi[:, g_:g_ + 1], scale=sp[:, g_:g_ + 1])
            aAff.inc(a.drain())
        conv_nt = 0 if env.get("p0_only") else NT
        if conv_nt:
            a.wait_ge(dnW.h, 1)
        for it in range(conv_nt):
            cvd = A.bf(P0_CVD[it % 2], 4096)
            if it >= 2:
                a.wait_ge(rCd2.h(it), 32 * (it // 2))
            rCd.dma(it, a.dma_start(out=cvd[:, :], in_=scr["wdT"][:, it * 128:(it + 1) * 128],
                                    transpose=True))
        rCd2.fin(a)
    block.scalar(_a)

    conv_nt2 = 0 if env.get("p0_only") else NT

    def _s(sy):
        def wr_tile(j):
            mj, tj, Kj, _ = tiles[j]
            sy.wait_ge(vAff.h, j + 1)
            sy.wait_ge(aAff.h, j + 1)
            rWr.dma(j, sy.dma_start(out=mats[mj][4][128 * tj:128 * (tj + 1), :],
                                    in_=A.bf(P0_WO[j % 2], Kj)[:, :]))

        for i, (mi, ti, K, NGk) in enumerate(tiles):
            st_ = mats[mi][3]
            sp = A.f32(P0_SP[i % 2], NGk)
            if i >= 2:
                sy.wait_ge(vPrep.h, i - 1)
                sy.wait_ge(aAff.h, i - 1)
            rSt.dma(i, sy.dma_start(out=sp[:, :], in_=st_[128 * ti:128 * (ti + 1), :]))
            if i >= 1:
                wr_tile(i - 1)
            if i == DNT:
                for s_ in rWr.s:
                    sy.wait_ge(s_.h, s_.n)
                sy.sem_inc(dnW.h, 1)
                dnW.n += 1
        wr_tile(ntot - 1)
        if ntot == DNT:
            for s_ in rWr.s:
                sy.wait_ge(s_.h, s_.n)
            sy.sem_inc(dnW.h, 1)
            dnW.n += 1
        for it in range(conv_nt2):
            cvd = A.bf(P0_CVD[it % 2], 4096)
            sy.wait_ge(rCd.h(it), 16 * (it // 2 + 1))
            for dc2 in range(2):
                lo = dc2 * 8
                dstw = scr["wdN"][lo:lo + 8, :, it * 256:(it + 1) * 256].rearrange(
                    "c p n -> p c n")
                srcw = cvd[:, dc2 * 2048:(dc2 + 1) * 2048]
                rCd2.dma(it, sy.dma_start(out=dstw, in_=srcw))
        for j2, (src, dst) in enumerate(jobs):
            sy.wait_ge(rCv.h(j2), 16 * (j2 // 2 + 1))
            cvt = A.bf(P0_CVT[j2 % 2], 4096)
            dst_r = dst.rearrange("(kt p) a -> p kt a", p=128)
            rCw.dma(j2, sy.dma_start(out=dst_r, in_=cvt[:, 0:dst_r.shape[1] * dst_r.shape[2]]))
        rWr.fin(sy)
        rCd2.fin(sy)
        rCw.fin(sy)
        dbg = env["dbg"]
        if "dzb" in dbg:
            sMiscD = sem("sMiscD")
            sy.wait_ge(vAff.h, len(tiles))
            sy.wait_ge(aAff.h, len(tiles))
            sMiscD.dma(sy.dma_start(out=dbg["dzb"][:, :], in_=A.i32(P0_ZB[0], 86)[:, :]))
            sMiscD.dma(sy.dma_start(out=dbg["dzf"][:, :], in_=A.f32(P0_ZF[0], 86)[:, :]))
            sMiscD.dma(sy.dma_start(out=dbg["dsp"][:, :], in_=A.f32(P0_SP[0], 86)[:, :]))
            sMiscD.dma(sy.dma_start(out=dbg["dbi"][:, :], in_=A.f32(P0_BI[0], 86)[:, :]))
            sy.wait_ge(sMiscD.h, sMiscD.n)
    block.sync(_s)



def _emit_ffn(env, src_x, dst_x, m0, m1, final):
    nc, block, sem, A, sb, ps, scr, out = (env["nc"], env["block"], env["sem"],
                                           env["A"], env["sb"], env["ps"],
                                           env["scr"], env["out"])
    tag = "f2" if final else "f1"
    M = m1 - m0
    NMC = M // 512
    chunks = list(range(NMC))
    src_off = -HALO if final else 0
    wgT, wuT, wdN, hh = scr["wgT"], scr["wuT"], scr["wdN"], scr["hh"]

    sGx = sem("fsGx")
    aSq = sem("faSq")
    pSq = sem("fpSq")
    vStat = sem("fvStat")
    aStat = sem("faStat")
    vMur = sem("fvMur")
    pBc = sem("fpBc")
    vLn = sem("fvLn")
    sWg = sem("fsWg")
    sWu = sem("fsWu")
    pGU = sem("fpGU")
    aSg = sem("faSg")
    vH = sem("fvH")
    rHw = Ring(sem, "frHw", 4)
    sHr = sem("fsHr")
    rWd2 = Ring(sem, "frWd2", 2)
    pDn = sem("fpDn")
    rGxt = Ring(sem, "frGxt", 4)
    vDn = sem("fvDn")
    rXw = Ring(sem, "frXw", 4)
    pTr = sem("fpTr")
    vTr = sem("fvTr")
    B = {s.h: s.n for s in [sGx, aSq, pSq, vStat, aStat, vMur, pBc, vLn, sWg, sWu,
                            pGU, aSg, vH, sHr, pDn, vDn, pTr, vTr]}
    rB = {}
    for r in (rHw, rWd2, rGxt, rXw):
        for s in r.s:
            rB[s.h] = s.n

    wins = []
    n0 = 0
    while n0 < I:
        nw = min(512, I - n0)
        wins.append((n0, nw))
        n0 += nw
    units = []
    for w, (n0_, nw) in enumerate(wins):
        for mc in range(NMC):
            for ntl in range(nw // 128):
                units.append((w, mc, n0_ // 128 + ntl))
    NU = len(units)
    win_units_end = {}
    cnt = 0
    for w, (n0_, nw) in enumerate(wins):
        cnt += NMC * (nw // 128)
        win_units_end[w] = cnt

    wg_ms = []
    wu_ms = []
    hread_ms = []
    wd2_ms = []

    xb = A.bf(F_XB, KT * 512)
    xln = A.bf(F_XLN, KT * M)
    sqb = [A.bf(F_SQ, 512), A.bf(F_SQ + 512, 512)]
    rbb = A.bf(F_RB, 512)
    mbb = A.bf(F_RB + 512, 512)
    tmpb = [A.bf(F_TMP, 512), A.bf(F_TMP + 512, 512)]
    wgw = A.bf(F_WGW, KT * 512)
    wuw = A.bf(F_WUW, KT * 512)
    sgb = [A.bf(F_SG, 512), A.bf(F_SG + 512, 512)]
    ttb = [A.bf(F_TT, 512), A.bf(F_TT + 512, 512)]
    hbs = [A.bf(F_HB + 512 * j, 512) for j in range(4)]
    hbuf = A.bf(F_HBUF, NT * 512)
    if final:
        wds = [A.bf(F_WDS, NT * 256)]
        pieceb = [A.f32(F_WDS + NT * 256 + 256 * j, 128) for j in range(4)]
    else:
        wds = [A.bf(F_WDS, NT * 256), A.bf(F_WDS + NT * 256, NT * 256)]
        pieceb = None
    xtl = [A.f32(F_XT + 1024 * j, 512) for j in range(4)]
    x1b = [A.f32(F_X1 + 1024 * j, 512) for j in range(2)]
    rows = A.f32(88576, 3072)
    r_mu = rows[0:1, 0:512]
    r_ex = rows[0:1, 512:1024]
    r_va = rows[0:1, 1024:1536]
    r_mr = rows[0:1, 1536:2048]
    r_ln = rows[0:1, 2048:2560]
    r_rs = rows[0:1, 2560:3072]

    psS, psQ, psB1, psB2 = ps[0], ps[1], ps[2], ps[3]
    psG = [ps[0], ps[1]]
    psU = [ps[2], ps[3]]
    psD = [ps[4], ps[5]]
    psT = [ps[6], ps[7]]

    def _g(g):
        for c in chunks:
            if c >= 1:
                g.wait_ge(vLn.h, B[vLn.h] + c)
            csrc = src_x[:, m0 + 512 * c + src_off: m0 + 512 * (c + 1) + src_off]
            sGx.dma(g.dma_start(out=xb[:, :],
                                in_=csrc.rearrange("(kt p) m -> p kt m", p=128)))
        for u, (w, mc, nt) in enumerate(units):
            g.wait_ge(vH.h, B[vH.h] + u + 1)
            rHw.dma(u, g.dma_start(out=hh[mc, :, nt * 512:(nt + 1) * 512],
                                   in_=hbs[u % 4][:, :]))
        for c in chunks:
            for dt_ in range(KT):
                idx = c * KT + dt_
                if idx >= 4:
                    g.wait_ge(vDn.h, B[vDn.h] + idx - 3)
                csrc = src_x[128 * dt_:128 * (dt_ + 1),
                             m0 + 512 * c + src_off: m0 + 512 * (c + 1) + src_off]
                rGxt.dma(idx, g.dma_start(out=xtl[idx % 4][:, :], in_=csrc))
        rHw.fin(g)
        rGxt.fin(g)
    block.gpsimd(_g)

    def _a(a):
        for c in chunks:
            a.wait_ge(sGx.h, B[sGx.h] + 16 * (c + 1))
            for kt in range(KT):
                idx = c * KT + kt
                if idx >= 2:
                    a.wait_ge(pSq.h, B[pSq.h] + idx - 1)
                aSq.inc(a.activation(out=sqb[kt % 2][:, :],
                                     in_=xb[:, kt * 512:(kt + 1) * 512], func=AF.Square))
            a.wait_ge(vStat.h, B[vStat.h] + c + 1)
            a.activation(out=r_ln, in_=r_va, func=AF.Ln, bias=sb["epsc"][0:1, 0:1])
            a.drain()
            aStat.inc(a.activation(out=r_rs, in_=r_ln, func=AF.Exp, scale=-0.5))
        u = 0
        for w, (n0_, nw) in enumerate(wins):
            while u < win_units_end[w]:
                a.wait_ge(pGU.h, B[pGU.h] + u + 1)
                if u >= 2:
                    a.wait_ge(vH.h, B[vH.h] + u - 1)
                aSg.inc(a.activation(out=sgb[u % 2][:, :], in_=psG[u % 2][:, :],
                                     func=AF.Sigmoid))
                u += 1
        nslab = 1 if final else 2
        a.wait_ge(vH.h, B[vH.h] + NU)
        rHw.fin(a)
        for c in chunks:
            for dc in range(16):
                i = c * 16 + dc
                if i >= nslab:
                    a.wait_ge(pDn.h, B[pDn.h] + 2 * (i - nslab + 1))
                rWd2.dma(i, a.dma_start(out=wds[i % nslab][:, :], in_=wdN[dc]))
    block.scalar(_a)

    def _s(sy):
        dbgm = env["dbg"]
        if ("dxln" + tag) in dbgm:
            sy.wait_ge(vLn.h, B[vLn.h] + NMC)
            sXw.dma(sy.dma_start(out=dbgm["dxln" + tag][:, :], in_=xln[:, :]))
        for w, (n0_, nw) in enumerate(wins):
            if w == 0:
                sy.wait_ge(vLn.h, B[vLn.h] + NMC)
            else:
                sy.wait_ge(pGU.h, B[pGU.h] + win_units_end[w - 1])
            for kt in range(KT):
                sWg.dma(sy.dma_start(out=wgw[:, kt * 512:kt * 512 + nw],
                                     in_=wgT[n0_:n0_ + nw, kt * 128:(kt + 1) * 128],
                                     transpose=True))
            wg_ms.append(sWg.n)
            for kt in range(KT):
                sWu.dma(sy.dma_start(out=wuw[:, kt * 512:kt * 512 + nw],
                                     in_=wuT[n0_:n0_ + nw, kt * 128:(kt + 1) * 128],
                                     transpose=True))
            wu_ms.append(sWu.n)
        for c in chunks:
            rHw.fin(sy)
            if c >= 1:
                sy.wait_ge(pDn.h, B[pDn.h] + 32 * c)
            sHr.dma(sy.dma_start(out=hbuf[:, :], in_=hh[c]))
            hread_ms.append(sHr.n)
            for dt_ in range(KT):
                idx = c * KT + dt_
                if not final:
                    sy.wait_ge(vDn.h, B[vDn.h] + idx + 1)
                    rXw.dma(idx, sy.dma_start(
                        out=dst_x[128 * dt_:128 * (dt_ + 1), m0 + 512 * c:m0 + 512 * (c + 1)],
                        in_=x1b[idx % 2][:, :]))
            if final:
                for dt_ in range(KT):
                    idx = c * KT + dt_
                    for j in range(4):
                        tj = idx * 4 + j
                        sy.wait_ge(vTr.h, B[vTr.h] + tj + 1)
                        mrow = c * 512 + j * 128
                        rXw.dma(tj, sy.dma_start(
                            out=out[mrow:mrow + 128, dt_ * 128:(dt_ + 1) * 128],
                            in_=pieceb[tj % 4][:, :]))
        rXw.fin(sy)
        sy.wait_ge(sHr.h, sHr.n)
    block.sync(_s)

    def _v(v):
        for c in chunks:
            v.wait_ge(pSq.h, B[pSq.h] + KT * (c + 1))
            v.tensor_single_scalar(r_mu, psS[0:1, :], 1.0 / D, ALU.mult)
            v.tensor_single_scalar(r_ex, psQ[0:1, :], 1.0 / D, ALU.mult)
            v.drain()
            v.tensor_mul(r_va, r_mu, r_mu)
            v.drain()
            v.tensor_sub(r_va, r_ex, r_va)
            vStat.inc(v.drain())
            v.wait_ge(aStat.h, B[aStat.h] + c + 1)
            v.tensor_mul(r_mr, r_mu, r_rs)
            vMur.inc(v.drain())
            v.wait_ge(pBc.h, B[pBc.h] + c + 1)
            v.tensor_copy(rbb[:, :], psB1[:, :])
            v.tensor_copy(mbb[:, :], psB2[:, :])
            v.drain()
            for kt in range(KT):
                v.tensor_mul(tmpb[kt % 2][:, :], xb[:, kt * 512:(kt + 1) * 512], rbb[:, :])
                v.drain()
                v.tensor_sub(xln[:, kt * M + 512 * c:kt * M + 512 * (c + 1)],
                             tmpb[kt % 2][:, :], mbb[:, :])
            vLn.inc(v.drain())
        for u in range(NU):
            v.wait_ge(aSg.h, B[aSg.h] + u + 1)
            if u >= 4:
                v.wait_ge(rHw.h(u), rB[rHw.h(u)] + 16 * (u // 4))
            v.tensor_mul(ttb[u % 2][:, :], psG[u % 2][:, :], sgb[u % 2][:, :])
            v.drain()
            vH.inc(v.tensor_mul(hbs[u % 4][:, :], ttb[u % 2][:, :], psU[u % 2][:, :]))
        for c in chunks:
            for dt_ in range(KT):
                idx = c * KT + dt_
                v.wait_ge(pDn.h, B[pDn.h] + idx + 1)
                v.wait_ge(rGxt.h(idx), rB[rGxt.h(idx)] + 16 * (idx // 4 + 1))
                if not final:
                    if idx >= 2:
                        v.wait_ge(rXw.h(idx - 2), rB[rXw.h(idx - 2)] + 16 * ((idx - 2) // 4 + 1))
                    vDn.inc(v.scalar_tensor_tensor(x1b[idx % 2][:, :], psD[idx % 2][:, :],
                                                   0.5, xtl[idx % 4][:, :],
                                                   ALU.mult, ALU.add))
                else:
                    if idx >= 2:
                        v.wait_ge(pTr.h, B[pTr.h] + 4 * (idx - 1))
                    vDn.inc(v.scalar_tensor_tensor(x1b[idx % 2][:, :], psD[idx % 2][:, :],
                                                   0.5, xtl[idx % 4][:, :],
                                                   ALU.mult, ALU.add))
                    for j in range(4):
                        tj = idx * 4 + j
                        v.wait_ge(pTr.h, B[pTr.h] + tj + 1)
                        if tj >= 4:
                            v.wait_ge(rXw.h(tj), rB[rXw.h(tj)] + 16 * (tj // 4))
                        vTr.inc(v.tensor_copy(pieceb[tj % 4][:, :],
                                              psT[tj % 2][0:128, 0:128]))
    block.vector(_v)

    def _t(t):
        for c in chunks:
            t.wait_ge(sGx.h, B[sGx.h] + 16 * (c + 1))
            if c >= 1:
                t.wait_ge(vStat.h, B[vStat.h] + c)
                t.wait_ge(vLn.h, B[vLn.h] + c)
            for kt in range(KT):
                idx = c * KT + kt
                t.matmul(psS[0:1, :], sb["ones_b"][:, :], xb[:, kt * 512:(kt + 1) * 512],
                         start=(kt == 0), stop=(kt == KT - 1))
                t.wait_ge(aSq.h, B[aSq.h] + idx + 1)
                pSq.inc(t.matmul(psQ[0:1, :], sb["ones_b"][:, :], sqb[kt % 2][:, :],
                                 start=(kt == 0), stop=(kt == KT - 1)))
            t.wait_ge(vMur.h, B[vMur.h] + c + 1)
            t.matmul(psB1[:, :], sb["ones_r"][:, :], r_rs, start=True, stop=True)
            pBc.inc(t.matmul(psB2[:, :], sb["ones_r"][:, :], r_mr, start=True, stop=True))
        t.wait_ge(vLn.h, B[vLn.h] + NMC)
        for u, (w, mc, nt) in enumerate(units):
            first_of_win = (u == 0) or (units[u - 1][0] != w)
            if first_of_win:
                t.wait_ge(sWg.h, wg_ms[w])
                t.wait_ge(sWu.h, wu_ms[w])
            if u >= 2:
                t.wait_ge(vH.h, B[vH.h] + u - 1)
            ntl = nt - wins[w][0] // 128
            for kt in range(KT):
                t.matmul(psG[u % 2][:, :], wgw[:, kt * 512 + ntl * 128:kt * 512 + (ntl + 1) * 128],
                         xln[:, kt * M + mc * 512:kt * M + (mc + 1) * 512],
                         start=(kt == 0), stop=(kt == KT - 1))
            for kt in range(KT):
                mm = t.matmul(psU[u % 2][:, :], wuw[:, kt * 512 + ntl * 128:kt * 512 + (ntl + 1) * 128],
                              xln[:, kt * M + mc * 512:kt * M + (mc + 1) * 512],
                              start=(kt == 0), stop=(kt == KT - 1))
            pGU.inc(mm)
        nslab = 1 if final else 2
        for c in chunks:
            t.wait_ge(sHr.h, hread_ms[c])
            for dc in range(16):
                i = c * 16 + dc
                t.wait_ge(rWd2.h(i), rB[rWd2.h(i)] + 16 * (i // 2 + 1))
                for d2 in range(2):
                    dt_ = dc * 2 + d2
                    idx = c * KT + dt_
                    if idx >= 2:
                        t.wait_ge(vDn.h, B[vDn.h] + idx - 1)
                    for it in range(NT):
                        mm = t.matmul(psD[idx % 2][:, :],
                                      wds[i % nslab][:, it * 256 + d2 * 128:it * 256 + (d2 + 1) * 128],
                                      hbuf[:, it * 512:(it + 1) * 512],
                                      start=(it == 0), stop=(it == NT - 1))
                    pDn.inc(mm)
                    if final:
                        for j in range(4):
                            tj = idx * 4 + j
                            if j == 0:
                                t.wait_ge(vDn.h, B[vDn.h] + idx + 1)
                            if tj >= 2:
                                t.wait_ge(vTr.h, B[vTr.h] + tj - 1)
                            pTr.inc(t.transpose(psT[tj % 2][0:128, 0:128],
                                                x1b[idx % 2][:, j * 128:(j + 1) * 128],
                                                sb["idm_s"][:, :]))
    block.tensor(_t)



AT_XLN2 = 0
AT_QRT = 16384
AT_ATT = 49152
AT_KRT = 81920
AT_VSB = 83456
AT_COS = 84992
AT_SIN = 86528
AT_KS = 88064
AT_PRB = 89088
AT_FB = 91136
AT_T2B = 92160
AT_T2R = 65536
AT_RROW = 88064
AT_WQS = 49152
AT_WOS = 0
AT_XT = 16384


def _emit_attn(env, dst_x2, lvl=4):
    NDI = {4: 64, 5: 4, 6: 32, 7: 34}.get(lvl, 0)
    nc, block, sem, A, sb, ps, scr, tin = (env["nc"], env["block"], env["sem"],
                                           env["A"], env["sb"], env["ps"],
                                           env["scr"], env["tin"])
    x1h, xlnh = scr["x1h"], scr["xlnh"]
    NMC = 3
    M = MLOC

    sGx = sem("atsGx")
    aSq = sem("ataSq")
    pSq = sem("atpSq")
    vStat = sem("atvStat")
    aStat = sem("atataStat")
    vMur = sem("atvMur")
    pBc = sem("atpBc")
    vLn = sem("atvLn")
    sXl = sem("atsXl")
    sCS = sem("atsCS")
    sWk = sem("atsWk")
    pKm = sem("atpKm")
    vKs = sem("atvKs")
    pR = sem("atpR")
    vRope = sem("atvRope")
    sQx = sem("atsQx")
    rWq = Ring(sem, "atrWq", 2)
    pS = sem("atpS")
    aSg = sem("ataSg")
    pWU = sem("atpWU")
    vF = sem("atvF")
    pF = sem("atpF")
    vBl = sem("atvBl")
    rWo = Ring(sem, "atrWo", 2)
    pO = sem("atpO")
    rGxt = Ring(sem, "atrGxt", 4)
    vO = sem("atvO")
    rX2 = Ring(sem, "atrX2", 2)

    xb = A.bf(F_XB, KT * 512)
    sqb = [A.bf(16384, 512), A.bf(16384 + 512, 512)]
    rbb = A.bf(17408, 512)
    mbb = A.bf(17408 + 512, 512)
    tmpb = [A.bf(18432, 512), A.bf(18432 + 512, 512)]
    rows = A.f32(19456, 3072)
    r_mu = rows[0:1, 0:512]
    r_ex = rows[0:1, 512:1024]
    r_va = rows[0:1, 1024:1536]
    r_mr = rows[0:1, 1536:2048]
    r_ln = rows[0:1, 2048:2560]
    r_rs = rows[0:1, 2560:3072]
    xln2 = A.bf(AT_XLN2, KT * 512)
    qrt = A.bf(AT_QRT, 32 * 1024)
    att = A.bf(AT_ATT, 32 * 1024)
    krt = A.bf(AT_KRT, 1536)
    vsb = A.bf(AT_VSB, 12 * 128)
    cosb = A.bf(AT_COS, 1536)
    sinb = A.bf(AT_SIN, 1536)
    ksb = [A.bf(AT_KS, 512), A.bf(AT_KS + 512, 512)]
    prb = A.bf(AT_PRB, 4 * 512)
    fbb = A.bf(AT_FB, 512)
    ttR = [A.bf(AT_T2R, 512), A.bf(AT_T2R + 512, 512)]
    ttB = [A.bf(AT_T2B, 512), A.bf(AT_T2B + 512, 512)]
    rrow = A.f32(AT_RROW, 512)
    wqsl = [A.bf(AT_WQS + 4096 * j, 4096) for j in range(2)]
    wosl = [A.bf(AT_WOS + 4096 * j, 4096) for j in range(2)]
    wkvs = A.bf(25600, 2 * 4096)
    xtl = [A.f32(AT_XT + 1024 * j, 512) for j in range(4)]
    x2b = [A.f32(AT_XT + 4096 + 1024 * j, 512) for j in range(2)]

    psS_, psQ_, psB1, psB2 = ps[0], ps[1], ps[2], ps[3]
    psSc = [ps[0], ps[1]]
    psD = [ps[2], ps[3]]
    psU = [ps[4], ps[5]]
    psP = ps[4]
    psP2 = ps[5]
    psR_ = ps[6]
    psO_ = [ps[6], ps[7]]

    def qspan(w):
        return max(256 * (w - 1), 0), min(256 * (w + 1), 1024)
    blends_before = []
    nb = 0
    for n in range(32 * 5):
        blends_before.append(nb)
        if n % 5 != 0:
            nb += 1

    def _g(g):
        sCS.dma(g.dma_start(out=cosb[:, :], in_=tin["cosd"][:, :]))
        sCS.dma(g.dma_start(out=sinb[:, :], in_=tin["sind"][:, :]))
        for c in range(NMC):
            if c >= 1:
                g.wait_ge(vLn.h, c)
            csrc = x1h[:, 512 * c:512 * (c + 1)]
            sGx.dma(g.dma_start(out=xb[:, :],
                                in_=csrc.rearrange("(kt p) m -> p kt m", p=128)))
        for qc in range(2):
            for dt_ in range(KT):
                idx = qc * KT + dt_
                if idx >= NDI:
                    continue
                if idx == 0:
                    g.wait_ge(vBl.h, 128)
                if idx >= 4:
                    g.wait_ge(vO.h, idx - 3)
                csrc = x1h[128 * dt_:128 * (dt_ + 1), HALO + 512 * qc:HALO + 512 * (qc + 1)]
                rGxt.dma(idx, g.dma_start(out=xtl[idx % 4][:, :], in_=csrc))
        rGxt.fin(g)
        g.wait_ge(sCS.h, sCS.n)
    block.gpsimd(_g)

    def _a(a):
        for c in range(NMC):
            a.wait_ge(sGx.h, 16 * (c + 1))
            for kt in range(KT):
                idx = c * KT + kt
                if idx >= 2:
                    a.wait_ge(pSq.h, idx - 1)
                aSq.inc(a.activation(out=sqb[kt % 2][:, :],
                                     in_=xb[:, kt * 512:(kt + 1) * 512], func=AF.Square))
            a.wait_ge(vStat.h, c + 1)
            a.activation(out=r_ln, in_=r_va, func=AF.Ln, bias=sb["epsc"][0:1, 0:1])
            a.drain()
            aStat.inc(a.activation(out=r_rs, in_=r_ln, func=AF.Exp, scale=-0.5))
        si = 0
        for n in range(32 * 5 if lvl >= 3 else 0):
            h_, w = divmod(n, 5)
            qs, qe = qspan(w)
            qn = qe - qs
            for kc in range(4):
                if n >= 1 and kc == 0:
                    a.wait_ge(pWU.h, n)
                a.wait_ge(pS.h, si + 1)
                aSg.inc(a.activation(out=prb[:, kc * 512:kc * 512 + qn],
                                     in_=psSc[kc % 2][:, 0:qn], func=AF.Sigmoid,
                                     scale=SCALE,
                                     bias=sb["smask_s"][:, 4 * w + kc:4 * w + kc + 1]))
                si += 1
    block.scalar(_a)

    def _s(sy):
        sWk.dma(sy.dma_start(out=wkvs[:, 0:4096],
                             in_=scr["wkb"].rearrange("(kt p) d -> p kt d", p=128)))
        sWk.dma(sy.dma_start(out=wkvs[:, 4096:8192],
                             in_=scr["wvb"].rearrange("(kt p) d -> p kt d", p=128)))
        for c in range(NMC):
            sy.wait_ge(vLn.h, c + 1)
            dst = xlnh[:, 512 * c:512 * (c + 1)].rearrange("(kt p) m -> p kt m", p=128)
            sXl.dma(sy.dma_start(out=dst, in_=xln2[:, :]))
        for qc in range(2 if lvl >= 2 else 0):
            sy.wait_ge(pKm.h, 5 * NMC)
            sy.wait_ge(sXl.h, 16 * NMC)
            if qc >= 1:
                sy.wait_ge(pKm.h, 5 * NMC + 32)
            src = xlnh[:, HALO + 512 * qc:HALO + 512 * (qc + 1)]
            sQx.dma(sy.dma_start(out=xln2[:, :],
                                 in_=src.rearrange("(kt p) m -> p kt m", p=128)))
            for at in range(32):
                ai = qc * 32 + at
                if ai >= 2:
                    sy.wait_ge(pKm.h, 5 * NMC + ai - 1)
                rWq.dma(ai, sy.dma_start(out=wqsl[ai % 2][:, :],
                                          in_=scr["wqb"][at].rearrange("(kt p) a -> p kt a", p=128)))
        NDIa = min(NDI, 64)
        for di in range(NDIa + 2):
            if di < NDIa:
                qc, dt_ = divmod(di, 32)
                if di == 0:
                    sy.wait_ge(vBl.h, 128)
                if di >= 2:
                    sy.wait_ge(pO.h, di - 1)
                rWo.dma(di, sy.dma_start(out=wosl[di % 2][:, :],
                                          in_=scr["wob"][dt_].rearrange("(at p) d -> p at d", p=128)))
            j = di - 2
            if 0 <= j < NDIa:
                qcj, dtj = divmod(j, 32)
                sy.wait_ge(vO.h, j + 1)
                rX2.dma(j, sy.dma_start(
                    out=dst_x2[128 * dtj:128 * (dtj + 1), 512 * qcj:512 * (qcj + 1)],
                    in_=x2b[j % 2][:, :]))
        rX2.fin(sy)
        sy.wait_ge(sXl.h, sXl.n)
    block.sync(_s)

    def _v(v):
        for c in range(NMC):
            v.wait_ge(pSq.h, KT * (c + 1))
            v.tensor_single_scalar(r_mu, psS_[0:1, :], 1.0 / D, ALU.mult)
            v.tensor_single_scalar(r_ex, psQ_[0:1, :], 1.0 / D, ALU.mult)
            v.drain()
            v.tensor_mul(r_va, r_mu, r_mu)
            v.drain()
            v.tensor_sub(r_va, r_ex, r_va)
            vStat.inc(v.drain())
            v.wait_ge(aStat.h, c + 1)
            v.tensor_mul(r_mr, r_mu, r_rs)
            vMur.inc(v.drain())
            v.wait_ge(pBc.h, c + 1)
            v.tensor_copy(rbb[:, :], psB1[:, :])
            v.tensor_copy(mbb[:, :], psB2[:, :])
            v.drain()
            if c >= 1:
                v.wait_ge(sXl.h, 16 * c)
            for kt in range(KT):
                v.tensor_mul(tmpb[kt % 2][:, :], xb[:, kt * 512:(kt + 1) * 512], rbb[:, :])
                v.drain()
                v.tensor_sub(xln2[:, kt * 512:(kt + 1) * 512], tmpb[kt % 2][:, :], mbb[:, :])
            vLn.inc(v.drain())
            km = 5 * c
            v.wait_ge(pKm.h, km + 1)
            v.tensor_copy(ksb[0][:, :], psP[:, 0:512])
            vKs.inc(v.drain())
            v.wait_ge(pR.h, c + 1)
            v.wait_ge(sCS.h, 32)
            v.tensor_mul(ttR[0][:, :], ksb[0][:, :], cosb[:, 512 * c:512 * (c + 1)])
            v.drain()
            v.tensor_mul(ttR[1][:, :], psR_[:, 0:512], sinb[:, 512 * c:512 * (c + 1)])
            v.drain()
            v.tensor_add(krt[:, 512 * c:512 * (c + 1)], ttR[0][:, :], ttR[1][:, :])
            v.drain()
            v.wait_ge(pKm.h, km + 5)
            for mt in range(4):
                v.tensor_copy(vsb[:, (4 * c + mt) * 128:(4 * c + mt + 1) * 128],
                              psP2[:, mt * 128:(mt + 1) * 128])
            vRope.inc(v.drain())
        for qc in range(2 if lvl >= 2 else 0):
            for at in range(32):
                ai = qc * 32 + at
                v.wait_ge(pKm.h, 5 * NMC + ai + 1)
                v.tensor_copy(ksb[ai % 2][:, :], psP[:, 0:512] if ai % 2 == 0 else psP2[:, 0:512])
                vKs.inc(v.drain())
                v.wait_ge(pR.h, NMC + ai + 1)
                v.tensor_mul(ttR[0][:, :], ksb[ai % 2][:, :],
                             cosb[:, HALO + 512 * qc:HALO + 512 * (qc + 1)])
                v.drain()
                v.tensor_mul(ttR[1][:, :], psR_[:, 0:512],
                             sinb[:, HALO + 512 * qc:HALO + 512 * (qc + 1)])
                v.drain()
                v.tensor_add(qrt[:, at * 1024 + 512 * qc:at * 1024 + 512 * qc + 512],
                             ttR[0][:, :], ttR[1][:, :])
                vRope.inc(v.drain())
        bl = 0
        for n in range(32 * 5 if lvl >= 3 else 0):
            h_, w = divmod(n, 5)
            if w == 0:
                continue
            b = w - 1
            qsP, qeP = qspan(w - 1)
            qsC, qeC = qspan(w)
            off1 = 256 * b - qsP
            off2 = 256 * b - qsC
            v.wait_ge(pWU.h, n + 1)
            v.reciprocal(rrow[0:1, 0:256], psD[(w - 1) % 2][0:1, off1:off1 + 256])
            v.drain()
            v.reciprocal(rrow[0:1, 256:512], psD[w % 2][0:1, off2:off2 + 256])
            v.drain()
            v.tensor_mul(sb["frow"][0:1, 0:256], sb["alm_s"][0:1, 256 * b:256 * (b + 1)],
                         rrow[0:1, 0:256])
            v.drain()
            v.tensor_mul(sb["frow"][0:1, 256:512], sb["alpha_s"][0:1, 256 * b:256 * (b + 1)],
                         rrow[0:1, 256:512])
            vF.inc(v.drain())
            v.wait_ge(pF.h, bl + 1)
            v.tensor_copy(fbb[:, :], psR_[:, 0:512])
            v.drain()
            v.tensor_mul(ttB[0][:, :256], psU[(w - 1) % 2][:, off1:off1 + 256], fbb[:, 0:256])
            v.drain()
            v.tensor_mul(ttB[1][:, :256], psU[w % 2][:, off2:off2 + 256], fbb[:, 256:512])
            v.drain()
            v.tensor_add(att[:, h_ * 1024 + 256 * b:h_ * 1024 + 256 * (b + 1)],
                         ttB[0][:, :256], ttB[1][:, :256])
            vBl.inc(v.drain())
            bl += 1
        for qc in range(2):
            for dt_ in range(32):
                di = qc * 32 + dt_
                if di >= NDI:
                    continue
                v.wait_ge(pO.h, di + 1)
                v.wait_ge(rGxt.h(di), 16 * (di // 4 + 1))
                if di >= 2:
                    v.wait_ge(rX2.h(di - 2), 16 * ((di - 2) // 2 + 1))
                vO.inc(v.tensor_add(x2b[di % 2][:, :], psO_[di % 2][:, 0:512],
                                    xtl[di % 4][:, :]))
    block.vector(_v)

    def _t(t):
        for c in range(NMC):
            t.wait_ge(sGx.h, 16 * (c + 1))
            if c >= 1:
                t.wait_ge(vStat.h, c)
                t.wait_ge(vLn.h, c)
            for kt in range(KT):
                idx = c * KT + kt
                t.matmul(psS_[0:1, :], sb["ones_b"][:, :], xb[:, kt * 512:(kt + 1) * 512],
                         start=(kt == 0), stop=(kt == KT - 1))
                t.wait_ge(aSq.h, idx + 1)
                pSq.inc(t.matmul(psQ_[0:1, :], sb["ones_b"][:, :], sqb[kt % 2][:, :],
                                 start=(kt == 0), stop=(kt == KT - 1)))
            t.wait_ge(vMur.h, c + 1)
            t.matmul(psB1[:, :], sb["ones_r"][:, :], r_rs, start=True, stop=True)
            pBc.inc(t.matmul(psB2[:, :], sb["ones_r"][:, :], r_mr, start=True, stop=True))
            t.wait_ge(vLn.h, c + 1)
            t.wait_ge(sWk.h, 32)
            if c >= 1:
                t.wait_ge(vKs.h, c)
                t.wait_ge(vRope.h, c)
            for kt in range(KT):
                mmk = t.matmul(psP[:, 0:512], wkvs[:, kt * 128:(kt + 1) * 128],
                               xln2[:, kt * 512:(kt + 1) * 512],
                               start=(kt == 0), stop=(kt == KT - 1))
            pKm.inc(mmk)
            t.wait_ge(vKs.h, c + 1)
            pR.inc(t.matmul(psR_[:, 0:512], sb["rmat_s"][:, :], ksb[0][:, :],
                            start=True, stop=True))
            for mt in range(4):
                for kt in range(KT):
                    mmv = t.matmul(psP2[:, mt * 128:(mt + 1) * 128],
                                   xln2[:, kt * 512 + mt * 128:kt * 512 + (mt + 1) * 128],
                                   wkvs[:, 4096 + kt * 128:4096 + (kt + 1) * 128],
                                   start=(kt == 0), stop=(kt == KT - 1))
                pKm.inc(mmv)
        for qc in range(2 if lvl >= 2 else 0):
            t.wait_ge(sQx.h, 16 * (qc + 1))
            for at in range(32):
                ai = qc * 32 + at
                t.wait_ge(rWq.h(ai), 16 * (ai // 2 + 1))
                if ai >= 2:
                    t.wait_ge(vKs.h, NMC + ai - 1)
                bank = psP if ai % 2 == 0 else psP2
                for kt in range(KT):
                    mmq = t.matmul(bank[:, 0:512], wqsl[ai % 2][:, kt * 128:(kt + 1) * 128],
                                   xln2[:, kt * 512:(kt + 1) * 512],
                                   start=(kt == 0), stop=(kt == KT - 1))
                pKm.inc(mmq)
                t.wait_ge(vKs.h, NMC + ai + 1)
                if ai >= 1:
                    t.wait_ge(vRope.h, NMC + ai)
                pR.inc(t.matmul(psR_[:, 0:512], sb["rmat_s"][:, :], ksb[ai % 2][:, :],
                                start=True, stop=True))
        si = 0
        bl = 0
        for n in range(32 * 5 if lvl >= 3 else 0):
            h_, w = divmod(n, 5)
            qs, qe = qspan(w)
            qn = qe - qs
            if n == 0:
                t.wait_ge(vRope.h, NMC + 64)
            t.wait_ge(vBl.h, blends_before[n])
            for kc in range(4):
                gsi = si + kc
                if gsi >= 2:
                    t.wait_ge(aSg.h, gsi - 1)
                pS.inc(t.matmul(psSc[kc % 2][:, 0:qn],
                                krt[:, 256 * w + 128 * kc:256 * w + 128 * (kc + 1)],
                                qrt[:, h_ * 1024 + qs:h_ * 1024 + qe],
                                start=True, stop=True))
            si += 4
            for kc in range(4):
                t.wait_ge(aSg.h, si - 4 + kc + 1)
                t.matmul(psD[w % 2][0:1, 0:qn], sb["ones_b"][:, :], prb[:, kc * 512:kc * 512 + qn],
                         start=(kc == 0), stop=(kc == 3))
                mmu = t.matmul(psU[w % 2][:, 0:qn], vsb[:, (2 * w + kc) * 128:(2 * w + kc + 1) * 128],
                               prb[:, kc * 512:kc * 512 + qn],
                               start=(kc == 0), stop=(kc == 3))
            pWU.inc(mmu)
            if w >= 1:
                t.wait_ge(vF.h, bl + 1)
                if bl >= 1:
                    t.wait_ge(vBl.h, bl)
                pF.inc(t.matmul(psR_[:, 0:512], sb["ones_rb"][:, :], sb["frow"][0:1, :],
                                start=True, stop=True))
                bl += 1
        for qc in range(2):
            for dt_ in range(32):
                di = qc * 32 + dt_
                if di >= NDI:
                    continue
                t.wait_ge(rWo.h(di), 16 * (di // 2 + 1))
                if di >= 2:
                    t.wait_ge(vO.h, di - 1)
                for at in range(32):
                    mmo = t.matmul(psO_[di % 2][:, 0:512], wosl[di % 2][:, at * 128:(at + 1) * 128],
                                   att[:, at * 1024 + 512 * qc:at * 1024 + 512 * (qc + 1)],
                                   start=(at == 0), stop=(at == 31))
                pO.inc(mmo)
    block.tensor(_t)



def _rope_tables(core):
    base = 10000.0
    inv_freq = 1.0 / base ** (np.arange(0, DH, 2, dtype=np.float64) / DH)
    pos0 = 1024 * core - HALO
    t = np.clip(np.arange(pos0, pos0 + MLOC, dtype=np.float64), 0, S - 1)
    f = np.outer(t, inv_freq)
    cos = np.zeros((MLOC, DH), dtype=np.float32)
    sin = np.zeros((MLOC, DH), dtype=np.float32)
    cos[:, 0::2] = np.cos(f)
    cos[:, 1::2] = np.cos(f)
    sin[:, 0::2] = np.sin(f)
    sin[:, 1::2] = np.sin(f)
    return np.ascontiguousarray(cos.T), np.ascontiguousarray(sin.T)


def _host_stage(inputs):
    x = np.asarray(inputs["L0_attn_x"])[0]
    xT = np.ascontiguousarray(x.T[SIGMA_D, :])

    shared = {}
    shared["qgb"] = _stage_wbytes(inputs["L0_ffn_gate_qweight"], SIGMA_D)
    shared["qub"] = _stage_wbytes(inputs["L0_ffn_up_qweight"], SIGMA_D)
    shared["qdb"] = _stage_wbytes(inputs["L0_ffn_down_qweight"], SIGMA_I)
    shared["zgb"] = _stage_zbytes(inputs["L0_ffn_gate_qzeros"])
    shared["zub"] = _stage_zbytes(inputs["L0_ffn_up_qzeros"])
    shared["zdb"] = _stage_zbytes(inputs["L0_ffn_down_qzeros"])
    shared["sg"] = np.ascontiguousarray(np.asarray(inputs["L0_ffn_gate_scales"], np.float32).T[SIGMA_I, :])
    shared["su"] = np.ascontiguousarray(np.asarray(inputs["L0_ffn_up_scales"], np.float32).T[SIGMA_I, :])
    shared["sd"] = np.ascontiguousarray(np.asarray(inputs["L0_ffn_down_scales"], np.float32).T[SIGMA_D, :])
    shared["wq"] = np.ascontiguousarray(np.asarray(inputs["L0_attn_Wq"], np.float32).T[SIGMA_D, :])
    shared["wk"] = np.ascontiguousarray(np.asarray(inputs["L0_attn_Wk"], np.float32).T[SIGMA_D, :])
    shared["wv"] = np.ascontiguousarray(np.asarray(inputs["L0_attn_Wv"], np.float32).T[SIGMA_D, :])
    shared["wo"] = np.ascontiguousarray(np.asarray(inputs["L0_attn_Wo"], np.float32)[SIGMA_D, :].T)

    RT = np.zeros((128, 128), dtype=np.float32)
    for i_ in range(64):
        RT[i_ + 64, i_] = -1.0
    for i_ in range(64, 128):
        RT[i_ - 64, i_] = 1.0
    shared["rmat"] = RT.astype(bf16)
    shared["idm"] = np.eye(128, dtype=np.float32)

    in_maps = []
    for c in range(NCORES):
        m = dict(shared)
        lo = 1024 * c - HALO
        pad_l = max(0, -lo)
        lo_c = max(0, lo)
        hi = 1024 * c + CEN + HALO
        pad_r = max(0, hi - S)
        hi_c = min(S, hi)
        xs = np.zeros((D, MLOC), dtype=np.float32)
        xs[:, pad_l:MLOC - pad_r] = xT[:, lo_c:hi_c]
        m["xt"] = xs
        cos, sin = _rope_tables(c)
        m["cosd"] = cos
        m["sind"] = sin
        al = np.tile(np.linspace(0.0, 1.0, 256, dtype=np.float32), (4, 1))
        if c == 0:
            al[0, :] = 1.0
        m["alphav"] = np.ascontiguousarray(al.reshape(1, 4 * 256))
        m["almv"] = np.ascontiguousarray((1.0 - al).reshape(1, 4 * 256))
        sm = np.zeros((5, 4, 128), dtype=np.float32)
        if c == NCORES - 1:
            sm[4, 2:, :] = MASKNEG
        m["smaskv"] = np.ascontiguousarray(sm.reshape(20, 128))
        in_maps.append(m)
    return in_maps


_SIGMA_D_INV = np.argsort(SIGMA_D)


def kernel(**inputs):
    in_maps = _host_stage(inputs)
    nc = build_program()
    res = run_bass_kernel_spmd(nc, in_maps, core_ids=list(range(NCORES)))
    parts = []
    for c in range(NCORES):
        o = res.results[c]["outq"]
        parts.append(o[:, _SIGMA_D_INV])
    return np.concatenate(parts, axis=0).reshape(1, S, D).astype(np.float32)


# version 02
# speedup vs baseline: 1.2356x; 1.2356x over previous
import math
from contextlib import ExitStack

import numpy as np
import ml_dtypes

import concourse.bass as bass
import concourse.mybir as mybir
from concourse.bass_utils import run_bass_kernel_spmd

bf16 = ml_dtypes.bfloat16
F32 = mybir.dt.float32
BF16 = mybir.dt.bfloat16
I32 = mybir.dt.int32
U8 = mybir.dt.uint8
AF = mybir.ActivationFunctionType
ALU = mybir.AluOpType

D = 4096
I = 11008
S = 8192
NCORES = 8
HALO = 256
CEN = 1024
MLOC = CEN + 2 * HALO
H = 32
DH = 128
KT = D // 128
NT = I // 128
G = 128
NG_GU = D // G
NG_DN = I // G
SCALE = 1.0 / math.sqrt(DH)
EPS = 1e-5
MASKNEG = -30.0

_INV = np.array([0, 2, 4, 6, 1, 3, 5, 7])


def _sigma(n):
    p = np.arange(128)
    h = p // 64
    c = (p % 64) // 4
    b = p % 4
    base = 8 * c + _INV[2 * b + h]
    out = np.empty(n, dtype=np.int64)
    for t in range(n // 128):
        out[t * 128:(t + 1) * 128] = 128 * t + base
    return out


SIGMA_D = _sigma(D)
SIGMA_I = _sigma(I)


def _stage_wbytes(qweight, sigma_k):
    q = np.ascontiguousarray(np.asarray(qweight)[sigma_k, :])
    b = q.view(np.uint8).reshape(q.shape[0], q.shape[1], 4)
    return np.ascontiguousarray(b.transpose(1, 2, 0))


def _stage_zbytes(qzeros):
    z = np.ascontiguousarray(np.asarray(qzeros).T)
    b = z.view(np.uint8).reshape(z.shape[0], z.shape[1], 4)
    return np.ascontiguousarray(b.transpose(0, 2, 1))


class Sem:
    def __init__(self, ctx, nc, name):
        self.h = ctx.enter_context(nc.semaphore(name))
        self.n = 0

    def inc(self, instr, k=1):
        instr.then_inc(self.h, k)
        self.n += k
        return self.n

    def dma(self, instr):
        return self.inc(instr, 16)


class Ring:

    def __init__(self, sem, name, n):
        self.s = [sem(f"{name}{j}") for j in range(n)]
        self.n = n

    def dma(self, slot, instr):
        return self.s[slot % self.n].dma(instr)

    def h(self, slot):
        return self.s[slot % self.n].h

    def fin(self, eng):
        for s in self.s:
            eng.wait_ge(s.h, s.n)


class Arena:
    def __init__(self, t, cols):
        self.t = t
        self.cols = cols

    def bf(self, off, n):
        assert off + n <= self.cols, (off, n)
        return self.t[:, off:off + n]

    def f32(self, off, n):
        assert off % 2 == 0 and off + 2 * n <= self.cols, (off, n)
        return self.t.bitcast(F32)[:, off // 2:off // 2 + n]

    def i32(self, off, n):
        assert off % 2 == 0 and off + 2 * n <= self.cols, (off, n)
        return self.t.bitcast(I32)[:, off // 2:off // 2 + n]


ARENA_COLS = 99328
P0_CVT = [0, 8192]
P0_WB = [16384, 16384 + 22016]
P0_WO = [16384 + 44032, 16384 + 44032 + 11008]
_z0 = 16384 + 44032 + 22016
P0_ZB = [_z0, _z0 + 172]
P0_ZF = [_z0 + 344, _z0 + 344 + 172]
P0_SP = [_z0 + 688, _z0 + 688 + 172]
P0_BI = [_z0 + 1032, _z0 + 1032 + 172]
P0_CVD = [_z0 + 1376, _z0 + 1376 + 4096]
F_XLN = 0
F_XB = 49152
F_SQ = 65536
F_RB = 66560
F_TMP = 67584
F_WGW = 68608
F_WUW = 85000
F_WUW = 68608 + 16384
F_SG = 84992 + 16384
F_WGW = 49152 + 16384
F_WGW = F_XB
F_WUW = F_WGW + 16384
F_SG = F_WUW + 16384
F_TT = F_SG + 1024
F_HB = F_TT + 1024
F_SQ = 86016
F_RB = 87040
F_TMP = 88064
F_HBUF = 0
F_WDS = 44032
F_XT = 88576
F_X1 = F_XT + 4096
F_OROW = 0
F_OROW = 66048
F_OROW = None

A_XLN2 = 0
A_QRT = 16384
A_ATT = 49152
A_KRT = 81920
A_VSB = 83456
A_COS = 84992
A_SIN = 86528
A_KS = 88064
A_PRB = 89088
A_WQS = 91136
A_FB = 99328 - 512
A_WOS = A_WQS
A_QX = A_XLN2


def build_program(stages=("p0", "ffn1", "attn", "ffn2"), dbg=None, dbg_weights=False, p0_only=None, attn_lvl=4):
    nc = bass.Bass("TRN2", target_bir_lowering=False)
    dt = nc.dram_tensor

    tin = {}
    tin["xt"] = dt("xt", [D, MLOC], F32, kind="ExternalInput").ap()
    tin["qgb"] = dt("qgb", [I // 8, 4, D], U8, kind="ExternalInput").ap()
    tin["qub"] = dt("qub", [I // 8, 4, D], U8, kind="ExternalInput").ap()
    tin["qdb"] = dt("qdb", [D // 8, 4, I], U8, kind="ExternalInput").ap()
    tin["zgb"] = dt("zgb", [I // 8, 4, NG_GU], U8, kind="ExternalInput").ap()
    tin["zub"] = dt("zub", [I // 8, 4, NG_GU], U8, kind="ExternalInput").ap()
    tin["zdb"] = dt("zdb", [D // 8, 4, NG_DN], U8, kind="ExternalInput").ap()
    tin["sg"] = dt("sg", [I, NG_GU], F32, kind="ExternalInput").ap()
    tin["su"] = dt("su", [I, NG_GU], F32, kind="ExternalInput").ap()
    tin["sd"] = dt("sd", [D, NG_DN], F32, kind="ExternalInput").ap()
    tin["wq"] = dt("wq", [D, D], F32, kind="ExternalInput").ap()
    tin["wk"] = dt("wk", [D, DH], F32, kind="ExternalInput").ap()
    tin["wv"] = dt("wv", [D, DH], F32, kind="ExternalInput").ap()
    tin["wo"] = dt("wo", [D, D], F32, kind="ExternalInput").ap()
    tin["cosd"] = dt("cosd", [128, MLOC], F32, kind="ExternalInput").ap()
    tin["sind"] = dt("sind", [128, MLOC], F32, kind="ExternalInput").ap()
    tin["rmat"] = dt("rmat", [128, 128], BF16, kind="ExternalInput").ap()
    tin["idm"] = dt("idm", [128, 128], F32, kind="ExternalInput").ap()
    tin["alphav"] = dt("alphav", [1, 4 * 256], F32, kind="ExternalInput").ap()
    tin["almv"] = dt("almv", [1, 4 * 256], F32, kind="ExternalInput").ap()
    tin["smaskv"] = dt("smaskv", [20, 128], F32, kind="ExternalInput").ap()

    out = dt("outq", [CEN, D], F32, kind="ExternalOutput").ap()

    wkind = "ExternalOutput" if dbg_weights else "Internal"
    scr = {}
    scr["wgT"] = dt("wgT", [I, D], BF16, kind=wkind).ap()
    scr["wuT"] = dt("wuT", [I, D], BF16, kind=wkind).ap()
    scr["wdT"] = dt("wdT", [D, I], BF16, kind=wkind).ap()
    scr["wdN"] = dt("wdN", [16, 128, NT * 256], BF16, kind=wkind).ap()
    scr["wqb"] = dt("wqb", [KT, D, 128], BF16, kind="Internal").ap()
    scr["wob"] = dt("wob", [KT, D, 128], BF16, kind="Internal").ap()
    scr["wkb"] = dt("wkb", [D, DH], BF16, kind="Internal").ap()
    scr["wvb"] = dt("wvb", [D, DH], BF16, kind="Internal").ap()
    scr["x1h"] = dt("x1h", [D, MLOC], F32, kind="Internal").ap()
    scr["x2h"] = dt("x2h", [D, CEN], F32, kind="Internal").ap()
    scr["xlnh"] = dt("xlnh", [D, MLOC], BF16, kind="Internal").ap()
    scr["hh"] = dt("hh", [3, 128, NT * 512], BF16, kind=wkind).ap()

    dbg_map = {}
    if dbg:
        for name, shp, dtp in dbg:
            dbg_map[name] = dt(name, shp, dtp, kind="ExternalOutput").ap()

    ctx = ExitStack()
    with ctx:
        A = Arena(ctx.enter_context(nc.sbuf_tensor("arena", [128, ARENA_COLS], BF16)),
                  ARENA_COLS)
        sb = {}
        sb["maskc"] = ctx.enter_context(nc.sbuf_tensor("maskc", [128, 1], I32))
        sb["hisc"] = ctx.enter_context(nc.sbuf_tensor("hisc", [128, 1], F32))
        sb["epsc"] = ctx.enter_context(nc.sbuf_tensor("epsc", [128, 1], F32))
        sb["ones_b"] = ctx.enter_context(nc.sbuf_tensor("ones_b", [128, 1], BF16))
        sb["ones_r"] = ctx.enter_context(nc.sbuf_tensor("ones_r", [1, 128], F32))
        sb["ones_rb"] = ctx.enter_context(nc.sbuf_tensor("ones_rb", [1, 128], BF16))
        sb["rmat_s"] = ctx.enter_context(nc.sbuf_tensor("rmat_s", [128, 128], BF16))
        sb["idm_s"] = ctx.enter_context(nc.sbuf_tensor("idm_s", [128, 128], F32))
        sb["smask_s"] = ctx.enter_context(nc.sbuf_tensor("smask_s", [128, 20], F32))
        sb["alpha_s"] = A.f32(94720, 1024)[0:1, :]
        sb["alm_s"] = A.f32(96768, 1024)[0:1, :]
        sb["frow"] = A.bf(98816, 512)[0:1, :]
        ps = [ctx.enter_context(nc.psum_tensor(f"pp{i}", [128, 512], F32)) for i in range(8)]

        sems = {}

        def sem(name):
            if name not in sems:
                sems[name] = Sem(ctx, nc, name)
            return sems[name]

        block = ctx.enter_context(nc.Block())
        env = dict(nc=nc, block=block, sem=sem, A=A, sb=sb, ps=ps, tin=tin,
                   scr=scr, out=out, dbg=dbg_map, p0_only=p0_only)

        PH = sem("PH")
        phase_no = [0]

        def barrier():
            phase_no[0] += 1
            target = 5 * phase_no[0]
            for e in ["tensor", "vector", "scalar", "sync", "gpsimd"]:
                def _f(eng):
                    if e in ("vector", "scalar", "gpsimd"):
                        eng.drain()
                    eng.sem_inc(PH.h, 1)
                    eng.wait_ge(PH.h, target)
                getattr(block, e)(_f)
            PH.n += 5

        _emit_consts(env)
        barrier()
        if "p0" in stages:
            _emit_p0(env)
            barrier()
        if "ffn1" in stages:
            dst = dbg_map.get("dx1", scr["x1h"])
            _emit_ffn(env, src_x=tin["xt"], dst_x=dst, m0=0, m1=MLOC, final=False)
            barrier()
        if "attn" in stages:
            dst2 = dbg_map.get("dx2", scr["x2h"])
            _emit_attn(env, dst_x2=dst2, lvl=attn_lvl)
            barrier()
        if "ffn2" in stages:
            _emit_ffn(env, src_x=scr["x2h"], dst_x=None, m0=HALO, m1=HALO + CEN,
                      final=True)
            barrier()

    return nc



def _emit_consts(env):
    block, sem, sb, tin = env["block"], env["sem"], env["sb"], env["tin"]
    sMisc = sem("sMisc")

    def _v(v):
        v.memset(sb["maskc"][0:64, :], 15)
        v.memset(sb["maskc"][64:128, :], 240)
        v.memset(sb["hisc"][0:64, :], 1.0)
        v.memset(sb["hisc"][64:128, :], 1.0 / 16.0)
        v.memset(sb["epsc"][:, :], EPS)
        v.memset(sb["ones_b"][:, :], 1.0)
        v.memset(sb["ones_r"][:, :], 1.0)
        v.memset(sb["ones_rb"][:, :], 1.0)
    block.vector(_v)

    def _s(sy):
        sMisc.dma(sy.dma_start(out=sb["rmat_s"][:, :], in_=tin["rmat"][:, :]))
        sMisc.dma(sy.dma_start(out=sb["idm_s"][:, :], in_=tin["idm"][:, :]))
        sMisc.dma(sy.dma_start(out=sb["alpha_s"][:, :], in_=tin["alphav"][:, :]))
        sMisc.dma(sy.dma_start(out=sb["alm_s"][:, :], in_=tin["almv"][:, :]))
        sMisc.dma(sy.dma_start(out=sb["smask_s"][:, :],
                               in_=tin["smaskv"].rearrange("r p -> p r"),
                               allow_slow_non_contiguous=True))
        sy.wait_ge(sMisc.h, sMisc.n)
    block.sync(_s)



def _emit_p0(env):
    nc, block, sem, A, sb, tin, scr = (env["nc"], env["block"], env["sem"],
                                       env["A"], env["sb"], env["tin"], env["scr"])
    rCv = Ring(sem, "rCv", 2)
    rCw = Ring(sem, "rCw", 2)
    rGd = Ring(sem, "rGd", 2)
    rSt = Ring(sem, "rSt", 2)
    vPrep = sem("vPrep")
    vAff = sem("vAff")
    aAff = sem("aAff")
    rWr = Ring(sem, "rWr", 2)
    dnW = sem("dnW")
    rCd = Ring(sem, "rCd", 2)
    rCd2 = Ring(sem, "rCd2", 2)

    mats = [
        ("dn", tin["qdb"], tin["zdb"], tin["sd"], scr["wdT"], D // 128, I, NG_DN),
        ("gt", tin["qgb"], tin["zgb"], tin["sg"], scr["wgT"], NT, D, NG_GU),
        ("up", tin["qub"], tin["zub"], tin["su"], scr["wuT"], NT, D, NG_GU),
    ]
    only = env.get("p0_only")
    if only:
        mats = [m for m in mats if m[0] == only[0]]
        mats = [(m[0], m[1], m[2], m[3], m[4], min(m[5], only[1]), m[6], m[7]) for m in mats]
    tiles = []
    for mi, m in enumerate(mats):
        for ti in range(m[5]):
            tiles.append((mi, ti, m[6], m[7]))
    ntot = len(tiles)
    DNT = D // 128

    jobs = []
    for at in range(KT):
        jobs.append((tin["wq"][:, at * 128:(at + 1) * 128], scr["wqb"][at]))
    for at in range(KT):
        jobs.append((tin["wo"][:, at * 128:(at + 1) * 128], scr["wob"][at]))
    jobs.append((tin["wk"][:, :], scr["wkb"]))
    jobs.append((tin["wv"][:, :], scr["wvb"]))
    if env.get("p0_only"):
        jobs = []

    def _g(g):
        for i, (mi, ti, K, NGk) in enumerate(tiles):
            qb, zb = mats[mi][1], mats[mi][2]
            if i >= 2:
                g.wait_ge(vAff.h, i - 1)
                g.wait_ge(aAff.h, i - 1)
            wb = A.i32(P0_WB[i % 2], K)
            zbuf = A.i32(P0_ZB[i % 2], NGk)
            qsr = qb[16 * ti:16 * ti + 16].rearrange("c b k -> (c b) k")
            rGd.dma(i, g.dma_start(out=wb[0:64, :], in_=qsr))
            rGd.dma(i, g.dma_start(out=wb[64:128, :], in_=qsr))
            zsr = zb[16 * ti:16 * ti + 16].rearrange("c b g -> (c b) g")
            rGd.dma(i, g.dma_start(out=zbuf[0:64, :], in_=zsr))
            rGd.dma(i, g.dma_start(out=zbuf[64:128, :], in_=zsr))
        for j, (src, dst) in enumerate(jobs):
            if j >= 2:
                g.wait_ge(rCw.h(j), 16 * (j // 2))
            cvt = A.bf(P0_CVT[j % 2], 4096)
            src_r = src.rearrange("(kt p) a -> p kt a", p=128)
            rCv.dma(j, g.dma_start(out=cvt[:, 0:src_r.shape[1] * src_r.shape[2]], in_=src_r))
        rGd.fin(g)
        rCv.fin(g)
    block.gpsimd(_g)

    def _v(v):
        for i, (mi, ti, K, NGk) in enumerate(tiles):
            wb = A.i32(P0_WB[i % 2], K)
            zbuf = A.i32(P0_ZB[i % 2], NGk)
            zf = A.f32(P0_ZF[i % 2], NGk)
            sp = A.f32(P0_SP[i % 2], NGk)
            bi = A.f32(P0_BI[i % 2], NGk)
            wout = A.bf(P0_WO[i % 2], K)
            v.wait_ge(rGd.h(i), 64 * (i // 2 + 1))
            v.wait_ge(rSt.h(i), 16 * (i // 2 + 1))
            if i >= 2:
                v.wait_ge(rWr.h(i), 16 * (i // 2))
                v.wait_ge(aAff.h, i - 1)
            v.tensor_single_scalar(wb[:, :], wb[:, :], sb["maskc"][:, 0:1], ALU.bitwise_and)
            v.tensor_single_scalar(zbuf[:, :], zbuf[:, :], sb["maskc"][:, 0:1], ALU.bitwise_and)
            v.tensor_single_scalar(sp[:, :], sp[:, :], sb["hisc"][:, 0:1], ALU.mult)
            v.drain()
            v.tensor_copy(zf[:, :], zbuf[:, :])
            v.drain()
            v.tensor_mul(bi[:, :], zf[:, :], sp[:, :])
            v.drain()
            v.tensor_single_scalar(bi[:, :], bi[:, :], -1.0, ALU.mult)
            vPrep.inc(v.drain())
            half = NGk // 2
            for g_ in range(half, NGk):
                v.tensor_scalar(wout[:, g_ * G:(g_ + 1) * G], wb[:, g_ * G:(g_ + 1) * G],
                                zf[:, g_:g_ + 1], sp[:, g_:g_ + 1], ALU.subtract, ALU.mult)
            vAff.inc(v.drain())
    block.vector(_v)

    def _a(a):
        for i, (mi, ti, K, NGk) in enumerate(tiles):
            wb = A.i32(P0_WB[i % 2], K)
            sp = A.f32(P0_SP[i % 2], NGk)
            bi = A.f32(P0_BI[i % 2], NGk)
            wout = A.bf(P0_WO[i % 2], K)
            a.wait_ge(vPrep.h, i + 1)
            if i >= 2:
                a.wait_ge(rWr.h(i), 16 * (i // 2))
            half = NGk // 2
            for g_ in range(half):
                a.activation(out=wout[:, g_ * G:(g_ + 1) * G], in_=wb[:, g_ * G:(g_ + 1) * G],
                             func=AF.Identity, bias=bi[:, g_:g_ + 1], scale=sp[:, g_:g_ + 1])
            aAff.inc(a.drain())
        conv_nt = 0 if env.get("p0_only") else NT
        if conv_nt:
            a.wait_ge(dnW.h, 1)
        for it in range(conv_nt):
            cvd = A.bf(P0_CVD[it % 2], 4096)
            if it >= 2:
                a.wait_ge(rCd2.h(it), 32 * (it // 2))
            rCd.dma(it, a.dma_start(out=cvd[:, :], in_=scr["wdT"][:, it * 128:(it + 1) * 128],
                                    transpose=True))
        rCd2.fin(a)
    block.scalar(_a)

    conv_nt2 = 0 if env.get("p0_only") else NT

    def _s(sy):
        def wr_tile(j):
            mj, tj, Kj, _ = tiles[j]
            sy.wait_ge(vAff.h, j + 1)
            sy.wait_ge(aAff.h, j + 1)
            rWr.dma(j, sy.dma_start(out=mats[mj][4][128 * tj:128 * (tj + 1), :],
                                    in_=A.bf(P0_WO[j % 2], Kj)[:, :]))

        for i, (mi, ti, K, NGk) in enumerate(tiles):
            st_ = mats[mi][3]
            sp = A.f32(P0_SP[i % 2], NGk)
            if i >= 2:
                sy.wait_ge(vPrep.h, i - 1)
                sy.wait_ge(aAff.h, i - 1)
            rSt.dma(i, sy.dma_start(out=sp[:, :], in_=st_[128 * ti:128 * (ti + 1), :]))
            if i >= 1:
                wr_tile(i - 1)
            if i == DNT:
                for s_ in rWr.s:
                    sy.wait_ge(s_.h, s_.n)
                sy.sem_inc(dnW.h, 1)
                dnW.n += 1
        wr_tile(ntot - 1)
        if ntot == DNT:
            for s_ in rWr.s:
                sy.wait_ge(s_.h, s_.n)
            sy.sem_inc(dnW.h, 1)
            dnW.n += 1
        for it in range(conv_nt2):
            cvd = A.bf(P0_CVD[it % 2], 4096)
            sy.wait_ge(rCd.h(it), 16 * (it // 2 + 1))
            for dc2 in range(2):
                lo = dc2 * 8
                dstw = scr["wdN"][lo:lo + 8, :, it * 256:(it + 1) * 256].rearrange(
                    "c p n -> p c n")
                srcw = cvd[:, dc2 * 2048:(dc2 + 1) * 2048]
                rCd2.dma(it, sy.dma_start(out=dstw, in_=srcw))
        for j2, (src, dst) in enumerate(jobs):
            sy.wait_ge(rCv.h(j2), 16 * (j2 // 2 + 1))
            cvt = A.bf(P0_CVT[j2 % 2], 4096)
            dst_r = dst.rearrange("(kt p) a -> p kt a", p=128)
            rCw.dma(j2, sy.dma_start(out=dst_r, in_=cvt[:, 0:dst_r.shape[1] * dst_r.shape[2]]))
        rWr.fin(sy)
        rCd2.fin(sy)
        rCw.fin(sy)
        dbg = env["dbg"]
        if "dzb" in dbg:
            sMiscD = sem("sMiscD")
            sy.wait_ge(vAff.h, len(tiles))
            sy.wait_ge(aAff.h, len(tiles))
            sMiscD.dma(sy.dma_start(out=dbg["dzb"][:, :], in_=A.i32(P0_ZB[0], 86)[:, :]))
            sMiscD.dma(sy.dma_start(out=dbg["dzf"][:, :], in_=A.f32(P0_ZF[0], 86)[:, :]))
            sMiscD.dma(sy.dma_start(out=dbg["dsp"][:, :], in_=A.f32(P0_SP[0], 86)[:, :]))
            sMiscD.dma(sy.dma_start(out=dbg["dbi"][:, :], in_=A.f32(P0_BI[0], 86)[:, :]))
            sy.wait_ge(sMiscD.h, sMiscD.n)
    block.sync(_s)



def _emit_ffn(env, src_x, dst_x, m0, m1, final):
    nc, block, sem, A, sb, ps, scr, out = (env["nc"], env["block"], env["sem"],
                                           env["A"], env["sb"], env["ps"],
                                           env["scr"], env["out"])
    tag = "f2" if final else "f1"
    M = m1 - m0
    NMC = M // 512
    chunks = list(range(NMC))
    src_off = -HALO if final else 0
    wgT, wuT, wdN, hh = scr["wgT"], scr["wuT"], scr["wdN"], scr["hh"]

    sGx = sem("fsGx")
    aSq = sem("faSq")
    pSq = sem("fpSq")
    vStat = sem("fvStat")
    aStat = sem("faStat")
    vMur = sem("fvMur")
    pBc = sem("fpBc")
    vLn = sem("fvLn")
    rWg = Ring(sem, "frWg", 2)
    rWu = Ring(sem, "frWu", 2)
    pGU = sem("fpGU")
    aSg = sem("faSg")
    vH = sem("fvH")
    rHw = Ring(sem, "frHw", 4)
    sHr = sem("fsHr")
    rWd2 = Ring(sem, "frWd2", 2)
    pDn = sem("fpDn")
    rGxt = Ring(sem, "frGxt", 4)
    vDn = sem("fvDn")
    rXw = Ring(sem, "frXw", 4)
    pTr = sem("fpTr")
    vTr = sem("fvTr")
    B = {s.h: s.n for s in [sGx, aSq, pSq, vStat, aStat, vMur, pBc, vLn,
                            pGU, aSg, vH, sHr, pDn, vDn, pTr, vTr]}
    rB = {}
    for r in (rHw, rWd2, rGxt, rXw, rWg, rWu):
        for s in r.s:
            rB[s.h] = s.n

    wins = []
    n0 = 0
    while n0 < I:
        nw = min(256, I - n0)
        wins.append((n0, nw))
        n0 += nw
    units = []
    for w, (n0_, nw) in enumerate(wins):
        for mc in range(NMC):
            for ntl in range(nw // 128):
                units.append((w, mc, n0_ // 128 + ntl))
    NU = len(units)
    win_units_end = {}
    cnt = 0
    for w, (n0_, nw) in enumerate(wins):
        cnt += NMC * (nw // 128)
        win_units_end[w] = cnt

    wg_ms = []
    wu_ms = []
    hread_ms = []
    wd2_ms = []

    xb = A.bf(F_XB, KT * 512)
    xln = A.bf(F_XLN, KT * M)
    sqb = [A.bf(F_SQ, 512), A.bf(F_SQ + 512, 512)]
    rbb = A.bf(F_RB, 512)
    mbb = A.bf(F_RB + 512, 512)
    tmpb = [A.bf(F_TMP, 512), A.bf(F_TMP + 512, 512)]
    wgw = [A.bf(F_WGW, KT * 256), A.bf(F_WGW + KT * 256, KT * 256)]
    wuw = [A.bf(F_WUW, KT * 256), A.bf(F_WUW + KT * 256, KT * 256)]
    sgb = [A.bf(F_SG, 512), A.bf(F_SG + 512, 512)]
    ttb = [A.bf(F_TT, 512), A.bf(F_TT + 512, 512)]
    hbs = [A.bf(F_HB + 512 * j, 512) for j in range(4)]
    hbuf = A.bf(F_HBUF, NT * 512)
    if final:
        wds = [A.bf(F_WDS, NT * 256)]
        pieceb = [A.f32(F_WDS + NT * 256 + 256 * j, 128) for j in range(4)]
    else:
        wds = [A.bf(F_WDS, NT * 256), A.bf(F_WDS + NT * 256, NT * 256)]
        pieceb = None
    xtl = [A.f32(F_XT + 1024 * j, 512) for j in range(4)]
    x1b = [A.f32(F_X1 + 1024 * j, 512) for j in range(2)]
    rows = A.f32(88576, 3072)
    r_mu = rows[0:1, 0:512]
    r_ex = rows[0:1, 512:1024]
    r_va = rows[0:1, 1024:1536]
    r_mr = rows[0:1, 1536:2048]
    r_ln = rows[0:1, 2048:2560]
    r_rs = rows[0:1, 2560:3072]

    psS, psQ, psB1, psB2 = ps[0], ps[1], ps[2], ps[3]
    psG = [ps[0], ps[1]]
    psU = [ps[2], ps[3]]
    psD = [ps[4], ps[5]]
    psT = [ps[6], ps[7]]

    def _g(g):
        for c in chunks:
            if c >= 1:
                g.wait_ge(vLn.h, B[vLn.h] + c)
            csrc = src_x[:, m0 + 512 * c + src_off: m0 + 512 * (c + 1) + src_off]
            sGx.dma(g.dma_start(out=xb[:, :],
                                in_=csrc.rearrange("(kt p) m -> p kt m", p=128)))
        for u, (w, mc, nt) in enumerate(units):
            g.wait_ge(vH.h, B[vH.h] + u + 1)
            rHw.dma(u, g.dma_start(out=hh[mc, :, nt * 512:(nt + 1) * 512],
                                   in_=hbs[u % 4][:, :]))
        for c in chunks:
            for dt_ in range(KT):
                idx = c * KT + dt_
                if idx >= 4:
                    g.wait_ge(vDn.h, B[vDn.h] + idx - 3)
                csrc = src_x[128 * dt_:128 * (dt_ + 1),
                             m0 + 512 * c + src_off: m0 + 512 * (c + 1) + src_off]
                rGxt.dma(idx, g.dma_start(out=xtl[idx % 4][:, :], in_=csrc))
        rHw.fin(g)
        rGxt.fin(g)
    block.gpsimd(_g)

    def _a(a):
        for c in chunks:
            a.wait_ge(sGx.h, B[sGx.h] + 16 * (c + 1))
            for kt in range(KT):
                idx = c * KT + kt
                if idx >= 2:
                    a.wait_ge(pSq.h, B[pSq.h] + idx - 1)
                aSq.inc(a.activation(out=sqb[kt % 2][:, :],
                                     in_=xb[:, kt * 512:(kt + 1) * 512], func=AF.Square))
            a.wait_ge(vStat.h, B[vStat.h] + c + 1)
            a.activation(out=r_ln, in_=r_va, func=AF.Ln, bias=sb["epsc"][0:1, 0:1])
            a.drain()
            aStat.inc(a.activation(out=r_rs, in_=r_ln, func=AF.Exp, scale=-0.5))
        u = 0
        for w, (n0_, nw) in enumerate(wins):
            while u < win_units_end[w]:
                a.wait_ge(pGU.h, B[pGU.h] + u + 1)
                if u >= 2:
                    a.wait_ge(vH.h, B[vH.h] + u - 1)
                aSg.inc(a.activation(out=sgb[u % 2][:, :], in_=psG[u % 2][:, :],
                                     func=AF.Sigmoid))
                u += 1
        nslab = 1 if final else 2
        a.wait_ge(vH.h, B[vH.h] + NU)
        rHw.fin(a)
        for c in chunks:
            for dc in range(16):
                i = c * 16 + dc
                if i >= nslab:
                    a.wait_ge(pDn.h, B[pDn.h] + 2 * (i - nslab + 1))
                rWd2.dma(i, a.dma_start(out=wds[i % nslab][:, :], in_=wdN[dc]))
    block.scalar(_a)

    def _s(sy):
        dbgm = env["dbg"]
        if ("dxln" + tag) in dbgm:
            sy.wait_ge(vLn.h, B[vLn.h] + NMC)
            sXw.dma(sy.dma_start(out=dbgm["dxln" + tag][:, :], in_=xln[:, :]))
        for w, (n0_, nw) in enumerate(wins):
            if w <= 1:
                sy.wait_ge(vLn.h, B[vLn.h] + NMC)
            else:
                sy.wait_ge(pGU.h, B[pGU.h] + win_units_end[w - 2])
            for kt in range(KT):
                rWg.dma(w, sy.dma_start(out=wgw[w % 2][:, kt * 256:kt * 256 + nw],
                                        in_=wgT[n0_:n0_ + nw, kt * 128:(kt + 1) * 128],
                                        transpose=True))
            wg_ms.append(rWg.s[w % 2].n)
            for kt in range(KT):
                rWu.dma(w, sy.dma_start(out=wuw[w % 2][:, kt * 256:kt * 256 + nw],
                                        in_=wuT[n0_:n0_ + nw, kt * 128:(kt + 1) * 128],
                                        transpose=True))
            wu_ms.append(rWu.s[w % 2].n)
        for c in chunks:
            rHw.fin(sy)
            if c >= 1:
                sy.wait_ge(pDn.h, B[pDn.h] + 32 * c)
            sHr.dma(sy.dma_start(out=hbuf[:, :], in_=hh[c]))
            hread_ms.append(sHr.n)
            for dt_ in range(KT):
                idx = c * KT + dt_
                if not final:
                    sy.wait_ge(vDn.h, B[vDn.h] + idx + 1)
                    rXw.dma(idx, sy.dma_start(
                        out=dst_x[128 * dt_:128 * (dt_ + 1), m0 + 512 * c:m0 + 512 * (c + 1)],
                        in_=x1b[idx % 2][:, :]))
            if final:
                for dt_ in range(KT):
                    idx = c * KT + dt_
                    for j in range(4):
                        tj = idx * 4 + j
                        sy.wait_ge(vTr.h, B[vTr.h] + tj + 1)
                        mrow = c * 512 + j * 128
                        rXw.dma(tj, sy.dma_start(
                            out=out[mrow:mrow + 128, dt_ * 128:(dt_ + 1) * 128],
                            in_=pieceb[tj % 4][:, :]))
        rXw.fin(sy)
        sy.wait_ge(sHr.h, sHr.n)
    block.sync(_s)

    def _v(v):
        for c in chunks:
            v.wait_ge(pSq.h, B[pSq.h] + KT * (c + 1))
            v.tensor_single_scalar(r_mu, psS[0:1, :], 1.0 / D, ALU.mult)
            v.tensor_single_scalar(r_ex, psQ[0:1, :], 1.0 / D, ALU.mult)
            v.drain()
            v.tensor_mul(r_va, r_mu, r_mu)
            v.drain()
            v.tensor_sub(r_va, r_ex, r_va)
            vStat.inc(v.drain())
            v.wait_ge(aStat.h, B[aStat.h] + c + 1)
            v.tensor_mul(r_mr, r_mu, r_rs)
            vMur.inc(v.drain())
            v.wait_ge(pBc.h, B[pBc.h] + c + 1)
            v.tensor_copy(rbb[:, :], psB1[:, :])
            v.tensor_copy(mbb[:, :], psB2[:, :])
            v.drain()
            for kt in range(KT):
                v.tensor_mul(tmpb[kt % 2][:, :], xb[:, kt * 512:(kt + 1) * 512], rbb[:, :])
                v.drain()
                v.tensor_sub(xln[:, kt * M + 512 * c:kt * M + 512 * (c + 1)],
                             tmpb[kt % 2][:, :], mbb[:, :])
            vLn.inc(v.drain())
        for u in range(NU):
            v.wait_ge(aSg.h, B[aSg.h] + u + 1)
            if u >= 4:
                v.wait_ge(rHw.h(u), rB[rHw.h(u)] + 16 * (u // 4))
            v.tensor_mul(ttb[u % 2][:, :], psG[u % 2][:, :], sgb[u % 2][:, :])
            v.drain()
            vH.inc(v.tensor_mul(hbs[u % 4][:, :], ttb[u % 2][:, :], psU[u % 2][:, :]))
        for c in chunks:
            for dt_ in range(KT):
                idx = c * KT + dt_
                v.wait_ge(pDn.h, B[pDn.h] + idx + 1)
                v.wait_ge(rGxt.h(idx), rB[rGxt.h(idx)] + 16 * (idx // 4 + 1))
                if not final:
                    if idx >= 2:
                        v.wait_ge(rXw.h(idx - 2), rB[rXw.h(idx - 2)] + 16 * ((idx - 2) // 4 + 1))
                    vDn.inc(v.scalar_tensor_tensor(x1b[idx % 2][:, :], psD[idx % 2][:, :],
                                                   0.5, xtl[idx % 4][:, :],
                                                   ALU.mult, ALU.add))
                else:
                    if idx >= 2:
                        v.wait_ge(pTr.h, B[pTr.h] + 4 * (idx - 1))
                    vDn.inc(v.scalar_tensor_tensor(x1b[idx % 2][:, :], psD[idx % 2][:, :],
                                                   0.5, xtl[idx % 4][:, :],
                                                   ALU.mult, ALU.add))
                    for j in range(4):
                        tj = idx * 4 + j
                        v.wait_ge(pTr.h, B[pTr.h] + tj + 1)
                        if tj >= 4:
                            v.wait_ge(rXw.h(tj), rB[rXw.h(tj)] + 16 * (tj // 4))
                        vTr.inc(v.tensor_copy(pieceb[tj % 4][:, :],
                                              psT[tj % 2][0:128, 0:128]))
    block.vector(_v)

    def _t(t):
        for c in chunks:
            t.wait_ge(sGx.h, B[sGx.h] + 16 * (c + 1))
            if c >= 1:
                t.wait_ge(vStat.h, B[vStat.h] + c)
                t.wait_ge(vLn.h, B[vLn.h] + c)
            for kt in range(KT):
                idx = c * KT + kt
                t.matmul(psS[0:1, :], sb["ones_b"][:, :], xb[:, kt * 512:(kt + 1) * 512],
                         start=(kt == 0), stop=(kt == KT - 1))
                t.wait_ge(aSq.h, B[aSq.h] + idx + 1)
                pSq.inc(t.matmul(psQ[0:1, :], sb["ones_b"][:, :], sqb[kt % 2][:, :],
                                 start=(kt == 0), stop=(kt == KT - 1)))
            t.wait_ge(vMur.h, B[vMur.h] + c + 1)
            t.matmul(psB1[:, :], sb["ones_r"][:, :], r_rs, start=True, stop=True)
            pBc.inc(t.matmul(psB2[:, :], sb["ones_r"][:, :], r_mr, start=True, stop=True))
        t.wait_ge(vLn.h, B[vLn.h] + NMC)
        for u, (w, mc, nt) in enumerate(units):
            first_of_win = (u == 0) or (units[u - 1][0] != w)
            if first_of_win:
                t.wait_ge(rWg.h(w), wg_ms[w])
                t.wait_ge(rWu.h(w), wu_ms[w])
            if u >= 2:
                t.wait_ge(vH.h, B[vH.h] + u - 1)
            ntl = nt - wins[w][0] // 128
            for kt in range(KT):
                t.matmul(psG[u % 2][:, :], wgw[w % 2][:, kt * 256 + ntl * 128:kt * 256 + (ntl + 1) * 128],
                         xln[:, kt * M + mc * 512:kt * M + (mc + 1) * 512],
                         start=(kt == 0), stop=(kt == KT - 1))
            for kt in range(KT):
                mm = t.matmul(psU[u % 2][:, :], wuw[w % 2][:, kt * 256 + ntl * 128:kt * 256 + (ntl + 1) * 128],
                              xln[:, kt * M + mc * 512:kt * M + (mc + 1) * 512],
                              start=(kt == 0), stop=(kt == KT - 1))
            pGU.inc(mm)
        nslab = 1 if final else 2
        for c in chunks:
            t.wait_ge(sHr.h, hread_ms[c])
            for dc in range(16):
                i = c * 16 + dc
                t.wait_ge(rWd2.h(i), rB[rWd2.h(i)] + 16 * (i // 2 + 1))
                for d2 in range(2):
                    dt_ = dc * 2 + d2
                    idx = c * KT + dt_
                    if idx >= 2:
                        t.wait_ge(vDn.h, B[vDn.h] + idx - 1)
                    for it in range(NT):
                        mm = t.matmul(psD[idx % 2][:, :],
                                      wds[i % nslab][:, it * 256 + d2 * 128:it * 256 + (d2 + 1) * 128],
                                      hbuf[:, it * 512:(it + 1) * 512],
                                      start=(it == 0), stop=(it == NT - 1))
                    pDn.inc(mm)
                    if final:
                        for j in range(4):
                            tj = idx * 4 + j
                            if j == 0:
                                t.wait_ge(vDn.h, B[vDn.h] + idx + 1)
                            if tj >= 2:
                                t.wait_ge(vTr.h, B[vTr.h] + tj - 1)
                            pTr.inc(t.transpose(psT[tj % 2][0:128, 0:128],
                                                x1b[idx % 2][:, j * 128:(j + 1) * 128],
                                                sb["idm_s"][:, :]))
    block.tensor(_t)



AT_XLN2 = 0
AT_QRT = 16384
AT_ATT = 49152
AT_KRT = 81920
AT_VSB = 83456
AT_COS = 84992
AT_SIN = 86528
AT_KS = 88064
AT_PRB = 89088
AT_FB = 91136
AT_T2B = 92160
AT_T2R = 65536
AT_RROW = 88064
AT_WQS = 49152
AT_WOS = 0
AT_XT = 16384


def _emit_attn(env, dst_x2, lvl=4):
    NDI = {4: 64, 5: 4, 6: 32, 7: 34}.get(lvl, 0)
    nc, block, sem, A, sb, ps, scr, tin = (env["nc"], env["block"], env["sem"],
                                           env["A"], env["sb"], env["ps"],
                                           env["scr"], env["tin"])
    x1h, xlnh = scr["x1h"], scr["xlnh"]
    NMC = 3
    M = MLOC

    sGx = sem("atsGx")
    aSq = sem("ataSq")
    pSq = sem("atpSq")
    vStat = sem("atvStat")
    aStat = sem("atataStat")
    vMur = sem("atvMur")
    pBc = sem("atpBc")
    vLn = sem("atvLn")
    sXl = sem("atsXl")
    sCS = sem("atsCS")
    sWk = sem("atsWk")
    pKm = sem("atpKm")
    vKs = sem("atvKs")
    pR = sem("atpR")
    vRope = sem("atvRope")
    sQx = sem("atsQx")
    rWq = Ring(sem, "atrWq", 2)
    pS = sem("atpS")
    aSg = sem("ataSg")
    pWU = sem("atpWU")
    vF = sem("atvF")
    pF = sem("atpF")
    vBl = sem("atvBl")
    rWo = Ring(sem, "atrWo", 2)
    pO = sem("atpO")
    rGxt = Ring(sem, "atrGxt", 4)
    vO = sem("atvO")
    rX2 = Ring(sem, "atrX2", 2)

    xb = A.bf(F_XB, KT * 512)
    sqb = [A.bf(16384, 512), A.bf(16384 + 512, 512)]
    rbb = A.bf(17408, 512)
    mbb = A.bf(17408 + 512, 512)
    tmpb = [A.bf(18432, 512), A.bf(18432 + 512, 512)]
    rows = A.f32(19456, 3072)
    r_mu = rows[0:1, 0:512]
    r_ex = rows[0:1, 512:1024]
    r_va = rows[0:1, 1024:1536]
    r_mr = rows[0:1, 1536:2048]
    r_ln = rows[0:1, 2048:2560]
    r_rs = rows[0:1, 2560:3072]
    xln2 = A.bf(AT_XLN2, KT * 512)
    qrt = A.bf(AT_QRT, 32 * 1024)
    att = A.bf(AT_ATT, 32 * 1024)
    krt = A.bf(AT_KRT, 1536)
    vsb = A.bf(AT_VSB, 12 * 128)
    cosb = A.bf(AT_COS, 1536)
    sinb = A.bf(AT_SIN, 1536)
    ksb = [A.bf(AT_KS, 512), A.bf(AT_KS + 512, 512)]
    prb = A.bf(AT_PRB, 4 * 512)
    fbb = A.bf(AT_FB, 512)
    ttR = [A.bf(AT_T2R, 512), A.bf(AT_T2R + 512, 512)]
    ttB = [A.bf(AT_T2B, 512), A.bf(AT_T2B + 512, 512)]
    rrow = A.f32(AT_RROW, 512)
    wqsl = [A.bf(AT_WQS + 4096 * j, 4096) for j in range(2)]
    wosl = [A.bf(AT_WOS + 4096 * j, 4096) for j in range(2)]
    wkvs = A.bf(25600, 2 * 4096)
    xtl = [A.f32(AT_XT + 1024 * j, 512) for j in range(4)]
    x2b = [A.f32(AT_XT + 4096 + 1024 * j, 512) for j in range(2)]

    psS_, psQ_, psB1, psB2 = ps[0], ps[1], ps[2], ps[3]
    psSc = [ps[0], ps[1]]
    psD = [ps[2], ps[3]]
    psU = [ps[4], ps[5]]
    psP = ps[4]
    psP2 = ps[5]
    psR_ = ps[6]
    psO_ = [ps[6], ps[7]]

    def qspan(w):
        return max(256 * (w - 1), 0), min(256 * (w + 1), 1024)
    blends_before = []
    nb = 0
    for n in range(32 * 5):
        blends_before.append(nb)
        if n % 5 != 0:
            nb += 1

    def _g(g):
        sCS.dma(g.dma_start(out=cosb[:, :], in_=tin["cosd"][:, :]))
        sCS.dma(g.dma_start(out=sinb[:, :], in_=tin["sind"][:, :]))
        for c in range(NMC):
            if c >= 1:
                g.wait_ge(vLn.h, c)
            csrc = x1h[:, 512 * c:512 * (c + 1)]
            sGx.dma(g.dma_start(out=xb[:, :],
                                in_=csrc.rearrange("(kt p) m -> p kt m", p=128)))
        for qc in range(2):
            for dt_ in range(KT):
                idx = qc * KT + dt_
                if idx >= NDI:
                    continue
                if idx == 0:
                    g.wait_ge(vBl.h, 128)
                if idx >= 4:
                    g.wait_ge(vO.h, idx - 3)
                csrc = x1h[128 * dt_:128 * (dt_ + 1), HALO + 512 * qc:HALO + 512 * (qc + 1)]
                rGxt.dma(idx, g.dma_start(out=xtl[idx % 4][:, :], in_=csrc))
        rGxt.fin(g)
        g.wait_ge(sCS.h, sCS.n)
    block.gpsimd(_g)

    def _a(a):
        for c in range(NMC):
            a.wait_ge(sGx.h, 16 * (c + 1))
            for kt in range(KT):
                idx = c * KT + kt
                if idx >= 2:
                    a.wait_ge(pSq.h, idx - 1)
                aSq.inc(a.activation(out=sqb[kt % 2][:, :],
                                     in_=xb[:, kt * 512:(kt + 1) * 512], func=AF.Square))
            a.wait_ge(vStat.h, c + 1)
            a.activation(out=r_ln, in_=r_va, func=AF.Ln, bias=sb["epsc"][0:1, 0:1])
            a.drain()
            aStat.inc(a.activation(out=r_rs, in_=r_ln, func=AF.Exp, scale=-0.5))
        si = 0
        for n in range(32 * 5 if lvl >= 3 else 0):
            h_, w = divmod(n, 5)
            qs, qe = qspan(w)
            qn = qe - qs
            for kc in range(4):
                if n >= 1 and kc == 0:
                    a.wait_ge(pWU.h, n)
                a.wait_ge(pS.h, si + 1)
                aSg.inc(a.activation(out=prb[:, kc * 512:kc * 512 + qn],
                                     in_=psSc[kc % 2][:, 0:qn], func=AF.Sigmoid,
                                     scale=SCALE,
                                     bias=sb["smask_s"][:, 4 * w + kc:4 * w + kc + 1]))
                si += 1
    block.scalar(_a)

    def _s(sy):
        sWk.dma(sy.dma_start(out=wkvs[:, 0:4096],
                             in_=scr["wkb"].rearrange("(kt p) d -> p kt d", p=128)))
        sWk.dma(sy.dma_start(out=wkvs[:, 4096:8192],
                             in_=scr["wvb"].rearrange("(kt p) d -> p kt d", p=128)))
        for c in range(NMC):
            sy.wait_ge(vLn.h, c + 1)
            dst = xlnh[:, 512 * c:512 * (c + 1)].rearrange("(kt p) m -> p kt m", p=128)
            sXl.dma(sy.dma_start(out=dst, in_=xln2[:, :]))
        for qc in range(2 if lvl >= 2 else 0):
            sy.wait_ge(pKm.h, 5 * NMC)
            sy.wait_ge(sXl.h, 16 * NMC)
            if qc >= 1:
                sy.wait_ge(pKm.h, 5 * NMC + 32)
            src = xlnh[:, HALO + 512 * qc:HALO + 512 * (qc + 1)]
            sQx.dma(sy.dma_start(out=xln2[:, :],
                                 in_=src.rearrange("(kt p) m -> p kt m", p=128)))
            for at in range(32):
                ai = qc * 32 + at
                if ai >= 2:
                    sy.wait_ge(pKm.h, 5 * NMC + ai - 1)
                rWq.dma(ai, sy.dma_start(out=wqsl[ai % 2][:, :],
                                          in_=scr["wqb"][at].rearrange("(kt p) a -> p kt a", p=128)))
        NDIa = min(NDI, 64)
        for di in range(NDIa + 2):
            if di < NDIa:
                qc, dt_ = divmod(di, 32)
                if di == 0:
                    sy.wait_ge(vBl.h, 128)
                if di >= 2:
                    sy.wait_ge(pO.h, di - 1)
                rWo.dma(di, sy.dma_start(out=wosl[di % 2][:, :],
                                          in_=scr["wob"][dt_].rearrange("(at p) d -> p at d", p=128)))
            j = di - 2
            if 0 <= j < NDIa:
                qcj, dtj = divmod(j, 32)
                sy.wait_ge(vO.h, j + 1)
                rX2.dma(j, sy.dma_start(
                    out=dst_x2[128 * dtj:128 * (dtj + 1), 512 * qcj:512 * (qcj + 1)],
                    in_=x2b[j % 2][:, :]))
        rX2.fin(sy)
        sy.wait_ge(sXl.h, sXl.n)
    block.sync(_s)

    def _v(v):
        for c in range(NMC):
            v.wait_ge(pSq.h, KT * (c + 1))
            v.tensor_single_scalar(r_mu, psS_[0:1, :], 1.0 / D, ALU.mult)
            v.tensor_single_scalar(r_ex, psQ_[0:1, :], 1.0 / D, ALU.mult)
            v.drain()
            v.tensor_mul(r_va, r_mu, r_mu)
            v.drain()
            v.tensor_sub(r_va, r_ex, r_va)
            vStat.inc(v.drain())
            v.wait_ge(aStat.h, c + 1)
            v.tensor_mul(r_mr, r_mu, r_rs)
            vMur.inc(v.drain())
            v.wait_ge(pBc.h, c + 1)
            v.tensor_copy(rbb[:, :], psB1[:, :])
            v.tensor_copy(mbb[:, :], psB2[:, :])
            v.drain()
            if c >= 1:
                v.wait_ge(sXl.h, 16 * c)
            for kt in range(KT):
                v.tensor_mul(tmpb[kt % 2][:, :], xb[:, kt * 512:(kt + 1) * 512], rbb[:, :])
                v.drain()
                v.tensor_sub(xln2[:, kt * 512:(kt + 1) * 512], tmpb[kt % 2][:, :], mbb[:, :])
            vLn.inc(v.drain())
            km = 5 * c
            v.wait_ge(pKm.h, km + 1)
            v.tensor_copy(ksb[0][:, :], psP[:, 0:512])
            vKs.inc(v.drain())
            v.wait_ge(pR.h, c + 1)
            v.wait_ge(sCS.h, 32)
            v.tensor_mul(ttR[0][:, :], ksb[0][:, :], cosb[:, 512 * c:512 * (c + 1)])
            v.drain()
            v.tensor_mul(ttR[1][:, :], psR_[:, 0:512], sinb[:, 512 * c:512 * (c + 1)])
            v.drain()
            v.tensor_add(krt[:, 512 * c:512 * (c + 1)], ttR[0][:, :], ttR[1][:, :])
            v.drain()
            v.wait_ge(pKm.h, km + 5)
            for mt in range(4):
                v.tensor_copy(vsb[:, (4 * c + mt) * 128:(4 * c + mt + 1) * 128],
                              psP2[:, mt * 128:(mt + 1) * 128])
            vRope.inc(v.drain())
        for qc in range(2 if lvl >= 2 else 0):
            for at in range(32):
                ai = qc * 32 + at
                v.wait_ge(pKm.h, 5 * NMC + ai + 1)
                v.tensor_copy(ksb[ai % 2][:, :], psP[:, 0:512] if ai % 2 == 0 else psP2[:, 0:512])
                vKs.inc(v.drain())
                v.wait_ge(pR.h, NMC + ai + 1)
                v.tensor_mul(ttR[0][:, :], ksb[ai % 2][:, :],
                             cosb[:, HALO + 512 * qc:HALO + 512 * (qc + 1)])
                v.drain()
                v.tensor_mul(ttR[1][:, :], psR_[:, 0:512],
                             sinb[:, HALO + 512 * qc:HALO + 512 * (qc + 1)])
                v.drain()
                v.tensor_add(qrt[:, at * 1024 + 512 * qc:at * 1024 + 512 * qc + 512],
                             ttR[0][:, :], ttR[1][:, :])
                vRope.inc(v.drain())
        bl = 0
        for n in range(32 * 5 if lvl >= 3 else 0):
            h_, w = divmod(n, 5)
            if w == 0:
                continue
            b = w - 1
            qsP, qeP = qspan(w - 1)
            qsC, qeC = qspan(w)
            off1 = 256 * b - qsP
            off2 = 256 * b - qsC
            v.wait_ge(pWU.h, n + 1)
            v.reciprocal(rrow[0:1, 0:256], psD[(w - 1) % 2][0:1, off1:off1 + 256])
            v.drain()
            v.reciprocal(rrow[0:1, 256:512], psD[w % 2][0:1, off2:off2 + 256])
            v.drain()
            v.tensor_mul(sb["frow"][0:1, 0:256], sb["alm_s"][0:1, 256 * b:256 * (b + 1)],
                         rrow[0:1, 0:256])
            v.drain()
            v.tensor_mul(sb["frow"][0:1, 256:512], sb["alpha_s"][0:1, 256 * b:256 * (b + 1)],
                         rrow[0:1, 256:512])
            vF.inc(v.drain())
            v.wait_ge(pF.h, bl + 1)
            v.tensor_copy(fbb[:, :], psR_[:, 0:512])
            v.drain()
            v.tensor_mul(ttB[0][:, :256], psU[(w - 1) % 2][:, off1:off1 + 256], fbb[:, 0:256])
            v.drain()
            v.tensor_mul(ttB[1][:, :256], psU[w % 2][:, off2:off2 + 256], fbb[:, 256:512])
            v.drain()
            v.tensor_add(att[:, h_ * 1024 + 256 * b:h_ * 1024 + 256 * (b + 1)],
                         ttB[0][:, :256], ttB[1][:, :256])
            vBl.inc(v.drain())
            bl += 1
        for qc in range(2):
            for dt_ in range(32):
                di = qc * 32 + dt_
                if di >= NDI:
                    continue
                v.wait_ge(pO.h, di + 1)
                v.wait_ge(rGxt.h(di), 16 * (di // 4 + 1))
                if di >= 2:
                    v.wait_ge(rX2.h(di - 2), 16 * ((di - 2) // 2 + 1))
                vO.inc(v.tensor_add(x2b[di % 2][:, :], psO_[di % 2][:, 0:512],
                                    xtl[di % 4][:, :]))
    block.vector(_v)

    def _t(t):
        for c in range(NMC):
            t.wait_ge(sGx.h, 16 * (c + 1))
            if c >= 1:
                t.wait_ge(vStat.h, c)
                t.wait_ge(vLn.h, c)
            for kt in range(KT):
                idx = c * KT + kt
                t.matmul(psS_[0:1, :], sb["ones_b"][:, :], xb[:, kt * 512:(kt + 1) * 512],
                         start=(kt == 0), stop=(kt == KT - 1))
                t.wait_ge(aSq.h, idx + 1)
                pSq.inc(t.matmul(psQ_[0:1, :], sb["ones_b"][:, :], sqb[kt % 2][:, :],
                                 start=(kt == 0), stop=(kt == KT - 1)))
            t.wait_ge(vMur.h, c + 1)
            t.matmul(psB1[:, :], sb["ones_r"][:, :], r_rs, start=True, stop=True)
            pBc.inc(t.matmul(psB2[:, :], sb["ones_r"][:, :], r_mr, start=True, stop=True))
            t.wait_ge(vLn.h, c + 1)
            t.wait_ge(sWk.h, 32)
            if c >= 1:
                t.wait_ge(vKs.h, c)
                t.wait_ge(vRope.h, c)
            for kt in range(KT):
                mmk = t.matmul(psP[:, 0:512], wkvs[:, kt * 128:(kt + 1) * 128],
                               xln2[:, kt * 512:(kt + 1) * 512],
                               start=(kt == 0), stop=(kt == KT - 1))
            pKm.inc(mmk)
            t.wait_ge(vKs.h, c + 1)
            pR.inc(t.matmul(psR_[:, 0:512], sb["rmat_s"][:, :], ksb[0][:, :],
                            start=True, stop=True))
            for mt in range(4):
                for kt in range(KT):
                    mmv = t.matmul(psP2[:, mt * 128:(mt + 1) * 128],
                                   xln2[:, kt * 512 + mt * 128:kt * 512 + (mt + 1) * 128],
                                   wkvs[:, 4096 + kt * 128:4096 + (kt + 1) * 128],
                                   start=(kt == 0), stop=(kt == KT - 1))
                pKm.inc(mmv)
        for qc in range(2 if lvl >= 2 else 0):
            t.wait_ge(sQx.h, 16 * (qc + 1))
            for at in range(32):
                ai = qc * 32 + at
                t.wait_ge(rWq.h(ai), 16 * (ai // 2 + 1))
                if ai >= 2:
                    t.wait_ge(vKs.h, NMC + ai - 1)
                bank = psP if ai % 2 == 0 else psP2
                for kt in range(KT):
                    mmq = t.matmul(bank[:, 0:512], wqsl[ai % 2][:, kt * 128:(kt + 1) * 128],
                                   xln2[:, kt * 512:(kt + 1) * 512],
                                   start=(kt == 0), stop=(kt == KT - 1))
                pKm.inc(mmq)
                t.wait_ge(vKs.h, NMC + ai + 1)
                if ai >= 1:
                    t.wait_ge(vRope.h, NMC + ai)
                pR.inc(t.matmul(psR_[:, 0:512], sb["rmat_s"][:, :], ksb[ai % 2][:, :],
                                start=True, stop=True))
        si = 0
        bl = 0
        for n in range(32 * 5 if lvl >= 3 else 0):
            h_, w = divmod(n, 5)
            qs, qe = qspan(w)
            qn = qe - qs
            if n == 0:
                t.wait_ge(vRope.h, NMC + 64)
            t.wait_ge(vBl.h, blends_before[n])
            for kc in range(4):
                gsi = si + kc
                if gsi >= 2:
                    t.wait_ge(aSg.h, gsi - 1)
                pS.inc(t.matmul(psSc[kc % 2][:, 0:qn],
                                krt[:, 256 * w + 128 * kc:256 * w + 128 * (kc + 1)],
                                qrt[:, h_ * 1024 + qs:h_ * 1024 + qe],
                                start=True, stop=True))
            si += 4
            for kc in range(4):
                t.wait_ge(aSg.h, si - 4 + kc + 1)
                t.matmul(psD[w % 2][0:1, 0:qn], sb["ones_b"][:, :], prb[:, kc * 512:kc * 512 + qn],
                         start=(kc == 0), stop=(kc == 3))
                mmu = t.matmul(psU[w % 2][:, 0:qn], vsb[:, (2 * w + kc) * 128:(2 * w + kc + 1) * 128],
                               prb[:, kc * 512:kc * 512 + qn],
                               start=(kc == 0), stop=(kc == 3))
            pWU.inc(mmu)
            if w >= 1:
                t.wait_ge(vF.h, bl + 1)
                if bl >= 1:
                    t.wait_ge(vBl.h, bl)
                pF.inc(t.matmul(psR_[:, 0:512], sb["ones_rb"][:, :], sb["frow"][0:1, :],
                                start=True, stop=True))
                bl += 1
        for qc in range(2):
            for dt_ in range(32):
                di = qc * 32 + dt_
                if di >= NDI:
                    continue
                t.wait_ge(rWo.h(di), 16 * (di // 2 + 1))
                if di >= 2:
                    t.wait_ge(vO.h, di - 1)
                for at in range(32):
                    mmo = t.matmul(psO_[di % 2][:, 0:512], wosl[di % 2][:, at * 128:(at + 1) * 128],
                                   att[:, at * 1024 + 512 * qc:at * 1024 + 512 * (qc + 1)],
                                   start=(at == 0), stop=(at == 31))
                pO.inc(mmo)
    block.tensor(_t)



def _rope_tables(core):
    base = 10000.0
    inv_freq = 1.0 / base ** (np.arange(0, DH, 2, dtype=np.float64) / DH)
    pos0 = 1024 * core - HALO
    t = np.clip(np.arange(pos0, pos0 + MLOC, dtype=np.float64), 0, S - 1)
    f = np.outer(t, inv_freq)
    cos = np.zeros((MLOC, DH), dtype=np.float32)
    sin = np.zeros((MLOC, DH), dtype=np.float32)
    cos[:, 0::2] = np.cos(f)
    cos[:, 1::2] = np.cos(f)
    sin[:, 0::2] = np.sin(f)
    sin[:, 1::2] = np.sin(f)
    return np.ascontiguousarray(cos.T), np.ascontiguousarray(sin.T)


def _host_stage(inputs):
    x = np.asarray(inputs["L0_attn_x"])[0]
    xT = np.ascontiguousarray(x.T[SIGMA_D, :])

    shared = {}
    shared["qgb"] = _stage_wbytes(inputs["L0_ffn_gate_qweight"], SIGMA_D)
    shared["qub"] = _stage_wbytes(inputs["L0_ffn_up_qweight"], SIGMA_D)
    shared["qdb"] = _stage_wbytes(inputs["L0_ffn_down_qweight"], SIGMA_I)
    shared["zgb"] = _stage_zbytes(inputs["L0_ffn_gate_qzeros"])
    shared["zub"] = _stage_zbytes(inputs["L0_ffn_up_qzeros"])
    shared["zdb"] = _stage_zbytes(inputs["L0_ffn_down_qzeros"])
    shared["sg"] = np.ascontiguousarray(np.asarray(inputs["L0_ffn_gate_scales"], np.float32).T[SIGMA_I, :])
    shared["su"] = np.ascontiguousarray(np.asarray(inputs["L0_ffn_up_scales"], np.float32).T[SIGMA_I, :])
    shared["sd"] = np.ascontiguousarray(np.asarray(inputs["L0_ffn_down_scales"], np.float32).T[SIGMA_D, :])
    shared["wq"] = np.ascontiguousarray(np.asarray(inputs["L0_attn_Wq"], np.float32).T[SIGMA_D, :])
    shared["wk"] = np.ascontiguousarray(np.asarray(inputs["L0_attn_Wk"], np.float32).T[SIGMA_D, :])
    shared["wv"] = np.ascontiguousarray(np.asarray(inputs["L0_attn_Wv"], np.float32).T[SIGMA_D, :])
    shared["wo"] = np.ascontiguousarray(np.asarray(inputs["L0_attn_Wo"], np.float32)[SIGMA_D, :].T)

    RT = np.zeros((128, 128), dtype=np.float32)
    for i_ in range(64):
        RT[i_ + 64, i_] = -1.0
    for i_ in range(64, 128):
        RT[i_ - 64, i_] = 1.0
    shared["rmat"] = RT.astype(bf16)
    shared["idm"] = np.eye(128, dtype=np.float32)

    in_maps = []
    for c in range(NCORES):
        m = dict(shared)
        lo = 1024 * c - HALO
        pad_l = max(0, -lo)
        lo_c = max(0, lo)
        hi = 1024 * c + CEN + HALO
        pad_r = max(0, hi - S)
        hi_c = min(S, hi)
        xs = np.zeros((D, MLOC), dtype=np.float32)
        xs[:, pad_l:MLOC - pad_r] = xT[:, lo_c:hi_c]
        m["xt"] = xs
        cos, sin = _rope_tables(c)
        m["cosd"] = cos
        m["sind"] = sin
        al = np.tile(np.linspace(0.0, 1.0, 256, dtype=np.float32), (4, 1))
        if c == 0:
            al[0, :] = 1.0
        m["alphav"] = np.ascontiguousarray(al.reshape(1, 4 * 256))
        m["almv"] = np.ascontiguousarray((1.0 - al).reshape(1, 4 * 256))
        sm = np.zeros((5, 4, 128), dtype=np.float32)
        if c == NCORES - 1:
            sm[4, 2:, :] = MASKNEG
        m["smaskv"] = np.ascontiguousarray(sm.reshape(20, 128))
        in_maps.append(m)
    return in_maps


_SIGMA_D_INV = np.argsort(SIGMA_D)


def kernel(**inputs):
    in_maps = _host_stage(inputs)
    nc = build_program()
    res = run_bass_kernel_spmd(nc, in_maps, core_ids=list(range(NCORES)))
    parts = []
    for c in range(NCORES):
        o = res.results[c]["outq"]
        parts.append(o[:, _SIGMA_D_INV])
    return np.concatenate(parts, axis=0).reshape(1, S, D).astype(np.float32)
